# version 06
# baseline (speedup 1.0000x reference)
import contextlib
import math
import numpy as np
import ml_dtypes
import concourse.bass as bass
import concourse.mybir as mybir
from concourse.bass_utils import run_bass_kernel_spmd

F32 = mybir.dt.float32
BF16 = mybir.dt.bfloat16
I32 = mybir.dt.int32
AF = mybir.ActivationFunctionType
ALU = mybir.AluOpType
AX = mybir.AxisListType
BF = ml_dtypes.bfloat16

E = 4096
DH = 128
HKV = 8
HQ = 32
GQ = 4
EC = 32
TB = 512
NCORE = 8
EPS = 1e-5
ISQD = 1.0 / math.sqrt(DH)


def full_geo(core):
    NB = 16
    own = [core, NB - 1 - core]
    A, B = own
    sched = [(0, A), (1, B)] + [(0, k) for k in range(A)] + [(1, k) for k in range(B)]
    return dict(NB=NB, T=NB * TB, own=own, sched=sched, NU=len(sched),
                DFF=11008, FC=86)



def _rope_tables(T):
    dt = np.float32
    theta, factor, hi_f, lo_f, octx = (dt(500000.0), dt(32.0), dt(4.0), dt(1.0),
                                       dt(8192.0))
    inv = (dt(1.0) / (theta ** (np.arange(0, DH, 2, dtype=dt) / dt(DH)))).astype(dt)
    wav = (dt(2.0) * dt(math.pi) / inv).astype(dt)
    inv = np.where(wav > octx / lo_f, inv / factor, inv).astype(dt)
    smooth = ((octx / wav - lo_f) / (hi_f - lo_f)).astype(dt)
    smoothed = ((dt(1.0) - smooth) * inv / factor + smooth * inv).astype(dt)
    is_med = (~(wav < octx / lo_f)) & (~(wav > octx / hi_f))
    inv = np.where(is_med, smoothed, inv).astype(dt)
    pos = np.arange(T, dtype=dt)
    freqs = pos[:, None] * inv[None, :]
    emb = np.concatenate([freqs, freqs], axis=-1)
    return np.cos(emb).astype(dt), np.sin(emb).astype(dt)


def host_shared(geo, x, wq, wk, wv, wo, gw, uw, dw, gs, us, ds):
    T, FC = geo["T"], geo["FC"]
    cos, sin = _rope_tables(T)
    sh = {}
    sh["x_full"] = np.ascontiguousarray(x.reshape(T, E))
    sh["cos_k"] = np.ascontiguousarray(cos.T)
    sh["sin_k"] = np.ascontiguousarray(sin.T)
    sh["wkT"] = np.ascontiguousarray(wk.T)
    sh["wvT"] = np.ascontiguousarray(wv.T)
    sh["wqTr"] = np.ascontiguousarray(wq.reshape(8, 512, EC, 128).transpose(0, 3, 2, 1))
    sh["woTr"] = np.ascontiguousarray(wo.reshape(8, 512, EC, 128).transpose(0, 3, 2, 1))
    sh["gateTr"] = np.ascontiguousarray(
        gw.reshape(FC, 128, EC, 128).transpose(0, 3, 2, 1))
    sh["upTr"] = np.ascontiguousarray(
        uw.reshape(FC, 128, EC, 128).transpose(0, 3, 2, 1))
    sh["downT"] = np.ascontiguousarray(dw.T)
    sh["gate_s"] = np.ascontiguousarray(gs.reshape(FC, 128).T)
    sh["up_s"] = np.ascontiguousarray(us.reshape(FC, 128).T)
    sh["down_sb"] = np.ascontiguousarray(np.broadcast_to(ds.reshape(1, E), (128, E)))
    sh["ident_b"] = np.eye(128, dtype=BF)
    sh["ident_f"] = np.eye(128, dtype=np.float32)
    sh["ones_c"] = np.ones((128, 128), dtype=BF)
    sh["ones_r"] = np.ones((1, 128), dtype=BF)
    P = np.zeros((128, 128), dtype=BF)
    for p in range(64):
        P[p + 64, p] = BF(-1.0)
        P[p, p + 64] = BF(1.0)
    sh["rotmat"] = P
    o = np.arange(4)[:, None, None] * 128 + np.arange(128)[None, :, None]
    i = np.arange(TB)[None, None, :]
    sh["maskT"] = np.ascontiguousarray((o <= i).astype(BF).transpose(1, 0, 2)
                                       .reshape(128, 4 * TB))
    lbc = np.zeros((8, 8 * 128), dtype=np.float32)
    for acix in range(8):
        lbc[acix, acix * 128:(acix + 1) * 128] = 1.0
    sh["lbc"] = lbc
    return sh


def host_percore(geo):
    NB, NU, own, sched = geo["NB"], geo["NU"], geo["own"], geo["sched"]
    p128 = np.arange(128, dtype=np.int32)
    ik = np.zeros((128, HKV * NU), np.int32)
    iv = np.zeros((128, HKV * NU * 4), np.int32)
    iq = np.zeros((128, HKV * GQ * NU), np.int32)
    for h in range(HKV):
        for u, (qsel, ksel) in enumerate(sched):
            ik[:, h * NU + u] = (h * NB + ksel) * 128 + p128
            for s in range(4):
                iv[:, (h * NU + u) * 4 + s] = \
                    (((ksel * 4 + s) * 128 + p128) * 8 + h)
            for qq in range(GQ):
                qh = h * GQ + qq
                iq[:, (h * NU + u) * GQ + qq] = (qh * 2 + qsel) * 128 + p128
    ixn = np.zeros((128, 2 * EC), np.int32)
    for a, b in enumerate(own):
        for ec in range(EC):
            ixn[:, a * EC + ec] = (b * EC + ec) * 128 + p128
    sel = np.zeros((128, NU * 2), np.float32)
    for u, (qsel, ksel) in enumerate(sched):
        sel[:, u * 2 + qsel] = 1.0
    lsel = np.zeros((128, NU * GQ * 8), ml_dtypes.bfloat16)
    for u, (qsel, ksel) in enumerate(sched):
        for qq in range(GQ):
            lsel[:, (u * GQ + qq) * 8 + qq * 2 + qsel] = 1.0
    return {"ik": ik, "iv": iv, "iq": iq, "ixn": ixn, "sel": sel,
            "lsel": lsel}



class S:
    def __init__(self, stack, nc, name):
        self.h = stack.enter_context(nc.semaphore(name))
        self.n = 0


def _barrier(nc, terms):
    if not terms:
        return
    with nc.Block() as blk:
        for attr in ("sync", "gpsimd", "scalar", "vector", "tensor"):
            def body(eng, _terms=tuple(terms)):
                for s in _terms:
                    eng.wait_ge(s.h, s.n)
            getattr(blk, attr)(body)



def build_program(geo, phases=("p1", "p1b", "p2", "p3a", "p3c"), dbg=False):
    NB, T, NU, FC = geo["NB"], geo["T"], geo["NU"], geo["FC"]
    nc = bass.Bass("TRN2", target_bir_lowering=False,
                   detect_race_conditions=False)

    def din(name, shape, dt):
        return nc.dram_tensor(name, list(shape), dt, kind="ExternalInput").ap()

    env = dict(nc=nc, geo=geo, dbg=dbg)
    env["x_full"] = din("x_full", (T, E), F32)
    env["x_own"] = din("x_own", (2 * TB, E), F32)
    env["cos_k"] = din("cos_k", (DH, T), F32)
    env["sin_k"] = din("sin_k", (DH, T), F32)
    env["cos_q"] = din("cos_q", (DH, 2 * TB), F32)
    env["sin_q"] = din("sin_q", (DH, 2 * TB), F32)
    env["wkT"] = din("wkT", (E, 1024), F32)
    env["wvT"] = din("wvT", (E, 1024), F32)
    env["wqTr"] = din("wqTr", (8, 128, EC, 512), F32)
    env["woTr"] = din("woTr", (8, 128, EC, 512), F32)
    env["gateTr"] = din("gateTr", (FC, 128, EC, 128), I32)
    env["upTr"] = din("upTr", (FC, 128, EC, 128), I32)
    env["downT"] = din("downT", (FC * 128, E), I32)
    env["gate_s"] = din("gate_s", (128, FC), F32)
    env["up_s"] = din("up_s", (128, FC), F32)
    env["down_sb"] = din("down_sb", (128, E), F32)
    ident_b = din("ident_b", (128, 128), BF16)
    ident_f = din("ident_f", (128, 128), F32)
    ones_c = din("ones_c", (128, 128), BF16)
    ones_r = din("ones_r", (1, 128), BF16)
    rotmat = din("rotmat", (128, 128), BF16)
    maskT = din("maskT", (128, 4 * TB), BF16)
    ik_t = din("ik", (128, HKV * NU), I32)
    iv_t = din("iv", (128, HKV * NU * 4), I32)
    iq_t = din("iq", (128, HKV * GQ * NU), I32)
    ixn_t = din("ixn", (128, 2 * EC), I32)
    sel_t = din("sel", (128, NU * 2), F32)
    lsel_t = din("lsel", (128, NU * GQ * 8), BF16)
    lbc_t = din("lbc", (8, 8 * 128), F32)

    env["out"] = nc.dram_tensor("out", [2 * TB, E], F32, kind="ExternalOutput").ap()

    skind = "ExternalOutput" if dbg else "Internal"
    env["k_scr"] = nc.dram_tensor("k_scr", [HKV * NB * 128, TB], BF16, kind=skind).ap()
    env["v_scr"] = nc.dram_tensor("v_scr", [NB * 4 * 128, 1024], BF16, kind=skind).ap()
    env["q_scr"] = nc.dram_tensor("q_scr", [HQ * 2 * 128, TB], BF16, kind=skind).ap()
    env["xn_scr"] = nc.dram_tensor("xn_scr", [NB * EC * 128, TB], BF16,
                                   kind=skind).ap()
    env["h_scr"] = nc.dram_tensor("h_scr", [FC * 128, 2 * TB], BF16, kind=skind).ap()
    env["ctx_scr"] = nc.dram_tensor("ctx_scr", [8 * HQ * 128, 128], BF16,
                                    kind=skind).ap()

    sb_const = nc.alloc_sbuf_tensor("sb_const", [128, 128 * 3], BF16)
    env["sb_identf"] = nc.alloc_sbuf_tensor("sb_identf", [128, 128], F32).ap()
    env["sb_onesr"] = nc.alloc_sbuf_tensor("sb_onesr", [1, 128], BF16).ap()
    env["sb_ixn"] = nc.alloc_sbuf_tensor("sb_ixn", [128, 2 * EC], I32).ap()
    env["sb_kn2"] = nc.alloc_sbuf_tensor("sb_kn2", [1, HKV], F32).ap()
    env["sb_qn2"] = nc.alloc_sbuf_tensor("sb_qn2", [1, HQ * 2], F32).ap()
    env["maskT_d"] = None


    env["IDB"] = sb_const.ap()[:, 0:128]
    env["ONESC"] = sb_const.ap()[:, 128:256]
    env["ROTM"] = sb_const.ap()[:, 256:384]

    gstk = contextlib.ExitStack()
    with gstk:
        env["gstk"] = gstk
        sC = S(gstk, nc, "sC")
        env["sC"] = sC

        env["maskT_d"] = maskT
        env["ik_d"] = ik_t
        env["iv_d"] = iv_t
        env["iq_d"] = iq_t
        env["sel_d"] = sel_t
        env["lsel_d"] = lsel_t
        env["lbc_d"] = lbc_t
        with nc.Block() as blk:
            @blk.sync
            def _(sync):
                for dst, src in ((sb_const.ap()[:, 0:128], ident_b),
                                 (sb_const.ap()[:, 128:256], ones_c),
                                 (sb_const.ap()[:, 256:384], rotmat),
                                 (env["sb_identf"][:, :], ident_f),
                                 (env["sb_onesr"][:, :], ones_r),
                                 (env["sb_ixn"][:, :], ixn_t)):
                    sync.dma_start(out=dst, in_=src).then_inc(sC.h, 16)
                sC.n = 6 * 16

        terms = [sC]
        if "p1" in phases:
            with contextlib.ExitStack() as pstk, nc.named_scope("p1"):
                env["pstk"] = pstk
                _barrier(nc, terms)
                terms += _phase1(env)
        if "p1b" in phases:
            with contextlib.ExitStack() as pstk, nc.named_scope("p1b"):
                env["pstk"] = pstk
                _barrier(nc, terms)
                terms += _phase1b(env)
        if "p2" in phases:
            with contextlib.ExitStack() as pstk, nc.named_scope("p2"):
                env["pstk"] = pstk
                _barrier(nc, terms)
                terms += _phase2(env)
        if "p3a" in phases:
            with contextlib.ExitStack() as pstk, nc.named_scope("p3a"):
                env["pstk"] = pstk
                _barrier(nc, terms)
                terms += _phase3a(env)
        if "p3c" in phases:
            with contextlib.ExitStack() as pstk, nc.named_scope("p3c"):
                env["pstk"] = pstk
                _barrier(nc, terms)
                terms += _phase3c(env)

        _barrier(nc, terms)

    return nc



def _phase1(env):
    nc, stk, geo, sC = env["nc"], env["pstk"], env["geo"], env["sC"]
    gstk = env["gstk"]
    NB = geo["NB"]
    NG = 4 * NB
    x_t = env["x_full"].rearrange("(g p) e -> g p e", p=128)
    xnw = env["xn_scr"].rearrange("(b ec p) t -> b p ec t", ec=EC, p=128)
    kw = env["k_scr"].rearrange("(h b p) t -> b p h t", h=HKV, p=128)
    vw = env["v_scr"]
    cos_k, sin_k = env["cos_k"], env["sin_k"]
    IDB, ONESC, ROTM = env["IDB"], env["ONESC"], env["ROTM"]
    sb_kn2 = env["sb_kn2"]

    xbuf = stk.enter_context(nc.sbuf_tensor("p1_xbuf", [128, 2 * E], BF16)).ap()
    xntok = stk.enter_context(nc.sbuf_tensor("p1_xntok", [128, 2 * E], BF16)).ap()
    stats = stk.enter_context(nc.sbuf_tensor("p1_stats", [128, 16], F32)).ap()
    xnfm2 = stk.enter_context(nc.sbuf_tensor("p1_xnfm", [128, 2 * EC * TB],
                                              BF16)).ap()
    wkres = stk.enter_context(nc.sbuf_tensor("p1_wk", [128, EC * 1024], BF16)).ap()
    wvch = stk.enter_context(nc.sbuf_tensor("p1_wv", [128, 2 * 4 * TB], BF16)).ap()
    csb = stk.enter_context(nc.sbuf_tensor("p1_cs", [128, 2 * TB], F32)).ap()
    kraw = stk.enter_context(nc.sbuf_tensor("p1_kraw", [128, HKV * TB], BF16)).ap()
    t1f = stk.enter_context(nc.sbuf_tensor("p1_t1", [128, TB], F32)).ap()
    t2f = stk.enter_context(nc.sbuf_tensor("p1_t2", [128, TB], F32)).ap()
    kstage = stk.enter_context(nc.sbuf_tensor("p1_kst", [128, HKV * TB], BF16)).ap()
    ksqb = stk.enter_context(nc.sbuf_tensor("p1_ksqb", [128, 4 * TB], BF16)).ap()
    vstage = stk.enter_context(nc.sbuf_tensor("p1_vst", [128, 8 * TB], BF16)).ap()
    sc11 = stk.enter_context(nc.sbuf_tensor("p1_sc11", [1, 16], F32)).ap()

    ptrA = stk.enter_context(nc.psum_tensor("p1_ptrA", [128, TB], BF16))
    pk = [stk.enter_context(nc.psum_tensor(f"p1_pk{i}", [128, TB], F32))
          for i in range(4)]
    prot = [stk.enter_context(nc.psum_tensor(f"p1_prot{i}", [128, TB], F32))
            for i in range(2)]
    pn = stk.enter_context(nc.psum_tensor("p1_pn", [1, TB], F32))

    sX = [S(gstk, nc, "p1_sX0"), S(gstk, nc, "p1_sX1")]
    sWk = S(gstk, nc, "p1_sWk")
    sWv = [S(gstk, nc, "p1_sWv0"), S(gstk, nc, "p1_sWv1")]
    sWvu = S(gstk, nc, "p1_sWvu")
    sCs = S(gstk, nc, "p1_sCs")
    sSq = S(gstk, nc, "p1_sSq")
    sLn = S(gstk, nc, "p1_sLn")
    sHm = S(gstk, nc, "p1_sHm")
    sCmb = S(gstk, nc, "p1_sCmb")
    sRst = S(gstk, nc, "p1_sRst")
    sNm = S(gstk, nc, "p1_sNm")
    sXn = S(gstk, nc, "p1_sXn")
    sTrp = S(gstk, nc, "p1_sTrp")
    sTre = S(gstk, nc, "p1_sTre")
    sKp = S(gstk, nc, "p1_sKp")
    sKe = S(gstk, nc, "p1_sKe")
    sRot = S(gstk, nc, "p1_sRot")
    sRp = S(gstk, nc, "p1_sRp")
    sSqb = S(gstk, nc, "p1_sSqb")
    sKn = S(gstk, nc, "p1_sKn")
    sKnv = S(gstk, nc, "p1_sKnv")
    sKw = S(gstk, nc, "p1_sKw")
    sVe = S(gstk, nc, "p1_sVe")
    sVw = S(gstk, nc, "p1_sVw")
    sXw = S(gstk, nc, "p1_sXw")
    sVc = S(gstk, nc, "p1_sVc")
    env["p1_sXw"] = sXw

    with nc.Block() as blk:
        @blk.gpsimd
        def _(gp):
            gp.dma_start(out=wkres.rearrange("p (ec d) -> p ec d", d=1024),
                         in_=env["wkT"].rearrange("(ec p) d -> p ec d", p=128)
                         ).then_inc(sWk.h, 16)
            sWk.n = 16
            wvr = env["wvT"].rearrange("(ec8 e2 p) d -> ec8 p e2 d", p=128, e2=4)
            for g in range(4):
                if g >= 2:
                    gp.wait_ge(sXn.h, g - 1)
                gp.dma_start(out=xbuf[:, (g % 2) * E:(g % 2 + 1) * E],
                             in_=x_t[g, :, :]).then_inc(sX[g % 2].h, 16)
            for b in range(NB):
                if b + 1 < NB:
                    for g in range(4 * b + 4, 4 * b + 8):
                        gp.wait_ge(sXn.h, g - 1)
                        gp.dma_start(out=xbuf[:, (g % 2) * E:(g % 2 + 1) * E],
                                     in_=x_t[g, :, :]).then_inc(sX[g % 2].h, 16)
                for w in range(2):
                    for c8 in range(8):
                        ch = b * 16 + w * 8 + c8
                        if ch >= 2:
                            gp.wait_ge(sWvu.h, ch - 1)
                        gp.dma_start(
                            out=wvch.rearrange("p (c e2 d) -> p c e2 d",
                                               c=2, e2=4)[:, ch % 2],
                            in_=wvr[c8, :, :, w * TB:(w + 1) * TB]
                        ).then_inc(sWv[ch % 2].h, 16)
            sX[0].n = (NG + 1) // 2 * 16
            sX[1].n = NG // 2 * 16
            sWv[0].n = NB * 8 * 16
            sWv[1].n = NB * 8 * 16

        @blk.sync
        def _(sync):
            for b in range(NB):
                if b > 0:
                    sync.wait_ge(sRp.h, 8 * b)
                sync.dma_start(out=csb[:, 0:TB],
                               in_=cos_k[:, b * TB:(b + 1) * TB]).then_inc(sCs.h, 16)
                sync.dma_start(out=csb[:, TB:2 * TB],
                               in_=sin_k[:, b * TB:(b + 1) * TB]).then_inc(sCs.h, 16)
                sync.wait_ge(sTre.h, 32 * (b + 1))
                xnfm_b = xnfm2[:, (b % 2) * EC * TB:(b % 2 + 1) * EC * TB]
                for q in range(4):
                    sync.dma_start(
                        out=xnw[b][:, q * 8:(q + 1) * 8, :],
                        in_=xnfm_b.rearrange("p (ec t) -> p ec t",
                                             t=TB)[:, q * 8:(q + 1) * 8, :]
                    ).then_inc(sXw.h, 16)
                sync.wait_ge(sRp.h, 8 * (b + 1))
                sync.dma_start(out=kw[b],
                               in_=kstage.rearrange("p (h t) -> p h t", t=TB)
                               ).then_inc(sKw.h, 16)
                for w in range(2):
                    for s in range(4):
                        sync.wait_ge(sVe.h, 8 * b + w * 4 + s + 1)
                        sync.dma_start(
                            out=vw[(b * 4 + s) * 128:(b * 4 + s + 1) * 128,
                                   w * TB:(w + 1) * TB],
                            in_=vstage[:, (w * 4 + s) * TB:(w * 4 + s + 1) * TB]
                        ).then_inc(sVw.h, 16)
            sCs.n = NB * 32
            sKw.n = NB * 16
            sVw.n = NB * 8 * 16
            sXw.n = NB * 4 * 16

        @blk.scalar
        def _(sc):
            for slot in range(NG + 5):
                if 1 <= slot <= NG:
                    gg = slot - 1
                    b = gg // 4
                    ii = gg % 4
                    xnfm_b = xnfm2[:, (b % 2) * EC * TB:(b % 2 + 1) * EC * TB]
                    for grp in range(8):
                        G = 8 * gg + grp
                        sc.wait_ge(sTrp.h, G + 1)
                        if b > 0 and ii == 0 and grp == 0:
                            sc.wait_ge(sWvu.h, 16 * (b - 1))
                            sc.wait_ge(sXw.h, 64 * b)
                        dst = xnfm_b.rearrange("p (ec t) -> p ec t", t=TB)[
                            :, 4 * grp:4 * grp + 4, ii * 128:(ii + 1) * 128]
                        sc.copy(dst, ptrA.ap().rearrange(
                            "p (j t) -> p j t", t=128)).then_inc(sTre.h, 1)
                if slot >= 8 and (slot - 8) % 4 == 0 and (slot - 8) // 4 < NB:
                    b = (slot - 8) // 4
                    for w in range(2):
                        for s in range(4):
                            sc.wait_ge(sWvu.h, 16 * b + 8 * (w + 1))
                            if b > 0:
                                sc.wait_ge(sVw.h, 128 * b)
                            sc.copy(vstage[:, (w * 4 + s) * TB:
                                            (w * 4 + s + 1) * TB],
                                    pk[s].ap()).then_inc(sVe.h, 1)
                if slot < NG:
                    g = slot
                    par = g % 2
                    xb = xbuf[:, par * E:(par + 1) * E]
                    xo = xntok[:, par * E:(par + 1) * E]
                    st = stats[:, par * 8:(par + 1) * 8]
                    sc.wait_ge(sX[g % 2].h, (g // 2 + 1) * 16)
                    if g >= 2:
                        sc.wait_ge(sCmb.h, g - 1)
                        sc.wait_ge(sTrp.h, 8 * (g - 1))
                    sc.activation(xo, xb, AF.Copy,
                                  accum_out=st[:, 0:1]).then_inc(sSq.h, 1)
                    sc.activation(xo, xb, AF.Square,
                                  accum_out=st[:, 1:2]).then_inc(sSq.h, 1)
                    sc.wait_ge(sCmb.h, g + 1)
                    if g >= 2:
                        sc.wait_ge(sHm.h, g - 1)
                    sc.activation(st[:, 4:5], st[:, 3:4],
                                  AF.Ln).then_inc(sLn.h, 1)
                    sc.wait_ge(sHm.h, g + 1)
                    if g >= 2:
                        sc.wait_ge(sXn.h, g - 1)
                    sc.activation(st[:, 5:6], st[:, 4:5],
                                  AF.Exp).then_inc(sRst.h, 1)
                for kgrp in range(2):
                    if (slot >= 4 + kgrp and (slot - 4 - kgrp) % 4 == 0
                            and (slot - 4 - kgrp) // 4 < NB):
                        b = (slot - 4 - kgrp) // 4
                        sc.wait_ge(sKp.h, 2 * b + kgrp + 1)
                        if b > 0:
                            sc.wait_ge(sRot.h, 8 * (b - 1) + 4 * (kgrp + 1))
                            sc.wait_ge(sRp.h, 8 * (b - 1) + 4 * (kgrp + 1))
                        for hh in range(4):
                            h = kgrp * 4 + hh
                            sc.copy(kraw[:, h * TB:(h + 1) * TB],
                                    pk[hh].ap()).then_inc(sKe.h, 1)
            sSq.n = NG * 2
            sLn.n = NG
            sRst.n = NG
            sTre.n = NG * 8
            sKe.n = NB * 8
            sVe.n = NB * 8

        @blk.vector
        def _(v):
            def vsync(ins):
                sVc.n += 1
                ins.then_inc(sVc.h, 1)
                v.wait_ge(sVc.h, sVc.n)

            def rope_head(b, h):
                idx = 8 * b + h
                par = h % 2
                v.wait_ge(sKe.h, idx + 1)
                v.wait_ge(sCs.h, 32 * b + 32)
                vsync(v.tensor_mul(t1f[:, :], kraw[:, h * TB:(h + 1) * TB],
                                   csb[:, 0:TB]))
                v.wait_ge(sRot.h, idx + 1)
                vsync(v.tensor_mul(t2f[:, :], prot[par].ap(),
                                   csb[:, TB:2 * TB]))
                if b > 0:
                    v.wait_ge(sKw.h, 16 * b)
                v.tensor_add(kstage[:, h * TB:(h + 1) * TB], t1f[:, :],
                             t2f[:, :]).then_inc(sRp.h, 1)

            for slot in range(NG + 5):
                if slot < NG:
                    g = slot
                    par = g % 2
                    xb = xbuf[:, par * E:(par + 1) * E]
                    xo = xntok[:, par * E:(par + 1) * E]
                    st = stats[:, par * 8:(par + 1) * 8]
                    v.wait_ge(sSq.h, 2 * (g + 1))
                    vsync(v.tensor_scalar_mul(st[:, 2:3], st[:, 0:1], 1.0 / E))
                    vsync(v.tensor_mul(st[:, 7:8], st[:, 2:3], st[:, 2:3]))
                    vsync(v.scalar_tensor_tensor(st[:, 3:4], st[:, 1:2],
                                                 1.0 / E, st[:, 7:8],
                                                 ALU.mult, ALU.subtract))
                    v.tensor_scalar_add(st[:, 3:4], st[:, 3:4],
                                        float(EPS)).then_inc(sCmb.h, 1)
                    v.wait_ge(sLn.h, g + 1)
                    v.tensor_scalar_mul(st[:, 4:5], st[:, 4:5],
                                        -0.5).then_inc(sHm.h, 1)
                    v.wait_ge(sRst.h, g + 1)
                    vsync(v.scalar_tensor_tensor(st[:, 6:7], st[:, 2:3],
                                                 -1.0, st[:, 5:6],
                                                 ALU.mult, ALU.mult))
                    v.tensor_scalar(xo, xb, st[:, 5:6], st[:, 6:7],
                                    ALU.mult, ALU.add).then_inc(sXn.h, 1)
                if slot >= 7 and (slot - 7) % 4 == 0 and (slot - 7) // 4 < NB:
                    b = (slot - 7) // 4
                    for h in range(8):
                        rope_head(b, h)
                    for half in range(2):
                        v.wait_ge(sRp.h, 8 * b + 4 * (half + 1))
                        if 2 * b + half >= 1:
                            v.wait_ge(sKn.h, 8 * b + 4 * half - 4)
                        v.tensor_mul(ksqb[:, :],
                                     kstage[:, half * 4 * TB:
                                            (half + 1) * 4 * TB],
                                     kstage[:, half * 4 * TB:
                                            (half + 1) * 4 * TB]
                                     ).then_inc(sSqb.h, 1)
                    for hn in range(HKV):
                        v.wait_ge(sKn.h, 8 * b + hn + 1)
                        vsync(v.reduce_max(sc11[:, 0:1], pn.ap(), axis=AX.X))
                        if b == 0:
                            v.tensor_copy(sb_kn2[:, hn:hn + 1],
                                          sc11[:, 0:1]).then_inc(sKnv.h, 1)
                        else:
                            v.tensor_max(sb_kn2[:, hn:hn + 1],
                                         sb_kn2[:, hn:hn + 1],
                                         sc11[:, 0:1]).then_inc(sKnv.h, 1)
            sCmb.n = NG
            sHm.n = NG
            sXn.n = NG
            sRp.n = NB * 8
            sSqb.n = NB * 2
            sKnv.n = NB * 8

        @blk.tensor
        def _(t):
            t.wait_ge(sWk.h, 16)

            def tr_tile(g):
                par = g % 2
                t.wait_ge(sXn.h, g + 1)
                for grp in range(8):
                    G = 8 * g + grp
                    if G >= 1:
                        t.wait_ge(sTre.h, G)
                    for j in range(4):
                        ins = t.transpose(
                            ptrA[:, j * 128:(j + 1) * 128],
                            xntok[:, par * E + (grp * 4 + j) * 128:
                                  par * E + (grp * 4 + j + 1) * 128],
                            IDB)
                    ins.then_inc(sTrp.h, 1)

            for g in range(4):
                tr_tile(g)
            for b in range(NB):
                t.wait_ge(sVe.h, 8 * b)
                t.wait_ge(sTre.h, 32 * (b + 1))
                xnfm = xnfm2[:, (b % 2) * EC * TB:(b % 2 + 1) * EC * TB]
                for grp in range(2):
                    if grp == 1:
                        t.wait_ge(sKe.h, 8 * b + 4)
                    for ec in range(EC):
                        for hh in range(4):
                            h = grp * 4 + hh
                            ins = t.matmul(pk[hh].ap(),
                                           wkres[:, ec * 1024 + h * 128:
                                                 ec * 1024 + (h + 1) * 128],
                                           xnfm[:, ec * TB:(ec + 1) * TB],
                                           start=(ec == 0), stop=(ec == EC - 1))
                    ins.then_inc(sKp.h, 1)
                if b + 1 < NB:
                    for g in range(4 * b + 4, 4 * b + 8):
                        tr_tile(g)
                for h in range(HKV):
                    idx = 8 * b + h
                    par = h % 2
                    t.wait_ge(sKe.h, idx + 1)
                    if idx >= 2:
                        t.wait_ge(sRp.h, idx - 1)
                    t.matmul(prot[par].ap(), ROTM,
                             kraw[:, h * TB:(h + 1) * TB], start=True,
                             stop=True).then_inc(sRot.h, 1)
                t.wait_ge(sKe.h, 8 * (b + 1))
                for w in range(2):
                    if w == 1:
                        t.wait_ge(sVe.h, 8 * b + 4)
                    for c8 in range(8):
                        if w == 0:
                            hn = c8
                            t.wait_ge(sSqb.h, 2 * b + hn // 4 + 1)
                            if 8 * b + hn >= 1:
                                t.wait_ge(sKnv.h, 8 * b + hn)
                            t.matmul(pn.ap(), ONESC[:, 0:1],
                                     ksqb[:, (hn % 4) * TB:(hn % 4 + 1) * TB],
                                     start=True, stop=True).then_inc(sKn.h, 1)
                        ch = b * 16 + w * 8 + c8
                        t.wait_ge(sWv[ch % 2].h, 16 * (ch // 2 + 1))
                        for e2 in range(4):
                            ec = 4 * c8 + e2
                            for s in range(4):
                                ins = t.matmul(
                                    pk[s].ap(),
                                    xnfm[:, ec * TB + s * 128:
                                         ec * TB + (s + 1) * 128],
                                    wvch[:, (ch % 2) * 4 * TB + e2 * TB:
                                         (ch % 2) * 4 * TB + (e2 + 1) * TB],
                                    start=(ec == 0), stop=(ec == EC - 1))
                        ins.then_inc(sWvu.h, 1)
            sTrp.n = NG * 8
            sKp.n = 2 * NB
            sRot.n = 8 * NB
            sKn.n = 8 * NB
            sWvu.n = 16 * NB

    return [sKw, sVw, sXw, sKnv, sKn, sSqb, sX[0], sX[1], sWv[0], sWv[1],
            sCs, sKe, sRp, sVe, sTre, sXn, sSq, sLn, sHm, sRst, sCmb, sTrp,
            sKp, sRot, sWvu, sWk, sVc]



def _phase1b(env):
    nc, stk, geo = env["nc"], env["pstk"], env["geo"]
    gstk = env["gstk"]
    IDB, ONESC, ROTM = env["IDB"], env["ONESC"], env["ROTM"]
    sb_qn2, sb_ixn = env["sb_qn2"], env["sb_ixn"]
    qw = env["q_scr"]

    xno = stk.enter_context(nc.sbuf_tensor("pb_xno", [128, 2 * EC * TB], BF16)).ap()
    wqc = stk.enter_context(nc.sbuf_tensor("pb_wqc", [128, 2 * EC * TB], BF16)).ap()
    csq = stk.enter_context(nc.sbuf_tensor("pb_csq", [128, 4 * TB], F32)).ap()
    qraw = stk.enter_context(nc.sbuf_tensor("pb_qraw", [128, 2 * TB], BF16)).ap()
    t1f = stk.enter_context(nc.sbuf_tensor("pb_t1", [128, TB], F32)).ap()
    t2f = stk.enter_context(nc.sbuf_tensor("pb_t2", [128, TB], F32)).ap()
    qstage = stk.enter_context(nc.sbuf_tensor("pb_qst", [128, 2 * TB], BF16)).ap()
    qsq = stk.enter_context(nc.sbuf_tensor("pb_qsq", [128, TB], BF16)).ap()
    sc11 = stk.enter_context(nc.sbuf_tensor("pb_sc11", [1, 16], F32)).ap()

    pqs = [stk.enter_context(nc.psum_tensor(f"pb_pq{i}", [128, TB], F32))
           for i in range(2)]
    prot = [stk.enter_context(nc.psum_tensor(f"pb_prot{i}", [128, TB], F32))
            for i in range(2)]
    pn = stk.enter_context(nc.psum_tensor("pb_pn", [1, TB], F32))

    sG = S(gstk, nc, "pb_sG")
    sW = [S(gstk, nc, "pb_sW0"), S(gstk, nc, "pb_sW1")]
    sCsq = S(gstk, nc, "pb_sCsq")
    sQp = S(gstk, nc, "pb_sQp")
    sQe = S(gstk, nc, "pb_sQe")
    sRot = S(gstk, nc, "pb_sRot")
    sRp = S(gstk, nc, "pb_sRp")
    sVc = S(gstk, nc, "pb_sVc")
    sQsq = S(gstk, nc, "pb_sQsq")
    sQn = S(gstk, nc, "pb_sQn")
    sQnv = S(gstk, nc, "pb_sQnv")
    sQw = [S(gstk, nc, "pb_sQw0"), S(gstk, nc, "pb_sQw1")]
    env["pb_sQw"] = sQw[0]

    with nc.Block() as blk:
        @blk.gpsimd
        def _(gp):
            for col in range(2 * EC):
                gp.indirect_dma_start(
                    xno[:, col * TB:(col + 1) * TB], None,
                    env["xn_scr"],
                    bass.IndirectOffsetOnAxis(sb_ixn[:, col:col + 1], 0)
                ).then_inc(sG.h, 16)
            sG.n = 2 * EC * 16
            for qc in range(8):
                if qc >= 2:
                    gp.wait_ge(sQp.h, (qc - 1) * 8)
                gp.dma_start(
                    out=wqc.rearrange("p (c ec d) -> p c ec d", c=2,
                                      d=TB)[:, qc % 2],
                    in_=env["wqTr"][qc]).then_inc(sW[qc % 2].h, 16)
            sW[0].n = 4 * 16
            sW[1].n = 4 * 16

        @blk.sync
        def _(sync):
            sync.dma_start(out=csq[:, 0:2 * TB],
                           in_=env["cos_q"]).then_inc(sCsq.h, 16)
            sync.dma_start(out=csq[:, 2 * TB:4 * TB],
                           in_=env["sin_q"]).then_inc(sCsq.h, 16)
            sCsq.n = 32
            for j in range(64):
                qh, a = divmod(j, 2)
                sync.wait_ge(sRp.h, j + 1)
                sync.dma_start(out=qw[(qh * 2 + a) * 128:(qh * 2 + a + 1) * 128, :],
                               in_=qstage[:, (j % 2) * TB:(j % 2 + 1) * TB]
                               ).then_inc(sQw[j % 2].h, 16)
            sQw[0].n = 32 * 16
            sQw[1].n = 32 * 16

        @blk.tensor
        def _(t):
            t.wait_ge(sG.h, 2 * EC * 16)
            def qn_mm(jj):
                t.wait_ge(sQsq.h, jj + 1)
                if jj >= 1:
                    t.wait_ge(sQnv.h, jj)
                t.matmul(pn.ap(), ONESC[:, 0:1], qsq[:, :], start=True,
                         stop=True).then_inc(sQn.h, 1)
            for qc in range(8):
                t.wait_ge(sW[qc % 2].h, 16 * (qc // 2 + 1))
                for hh in range(4):
                    for a in range(2):
                        j = qc * 8 + hh * 2 + a
                        if j >= 1:
                            qn_mm(j - 1)
                        pb = pqs[j % 2]
                        if j >= 2:
                            t.wait_ge(sQe.h, j - 1)
                        for ec in range(EC):
                            ins = t.matmul(
                                pb.ap(),
                                wqc[:, (qc % 2) * EC * TB + ec * TB + hh * 128:
                                    (qc % 2) * EC * TB + ec * TB + (hh + 1) * 128],
                                xno[:, (a * EC + ec) * TB:(a * EC + ec + 1) * TB],
                                start=(ec == 0), stop=(ec == EC - 1))
                        ins.then_inc(sQp.h, 1)
                        t.wait_ge(sQe.h, j + 1)
                        if j >= 2:
                            t.wait_ge(sRp.h, j - 1)
                        t.matmul(prot[j % 2].ap(), ROTM,
                                 qraw[:, (j % 2) * TB:(j % 2 + 1) * TB],
                                 start=True, stop=True).then_inc(sRot.h, 1)
            qn_mm(63)
            sQp.n = 64
            sRot.n = 64
            sQn.n = 64

        @blk.vector
        def _(v):
            def vsync(ins):
                sVc.n += 1
                ins.then_inc(sVc.h, 1)
                v.wait_ge(sVc.h, sVc.n)
            v.wait_ge(sCsq.h, 32)
            for j in range(64):
                a = j % 2
                pb = pqs[j % 2]
                v.wait_ge(sQp.h, j + 1)
                if j >= 2:
                    v.wait_ge(sRot.h, j - 1)
                v.tensor_copy(qraw[:, (j % 2) * TB:(j % 2 + 1) * TB],
                              pb.ap()).then_inc(sQe.h, 1)
                v.wait_ge(sQe.h, j + 1)
                vsync(v.tensor_mul(t1f[:, :],
                                   qraw[:, (j % 2) * TB:(j % 2 + 1) * TB],
                                   csq[:, a * TB:(a + 1) * TB]))
                v.wait_ge(sRot.h, j + 1)
                vsync(v.tensor_mul(t2f[:, :], prot[j % 2].ap(),
                                   csq[:, 2 * TB + a * TB:2 * TB + (a + 1) * TB]))
                if j >= 2:
                    v.wait_ge(sQw[j % 2].h, 16 * (j // 2))
                v.tensor_add(qstage[:, (j % 2) * TB:(j % 2 + 1) * TB],
                             t1f[:, :], t2f[:, :]).then_inc(sRp.h, 1)
                if j >= 1:
                    v.wait_ge(sQn.h, j)
                    vsync(v.reduce_max(sc11[:, 0:1], pn.ap(), axis=AX.X))
                    v.tensor_copy(sb_qn2[:, j - 1:j],
                                  sc11[:, 0:1]).then_inc(sQnv.h, 1)
                v.wait_ge(sRp.h, j + 1)
                if j >= 1:
                    v.wait_ge(sQn.h, j)
                v.tensor_mul(qsq[:, :],
                             qstage[:, (j % 2) * TB:(j % 2 + 1) * TB],
                             qstage[:, (j % 2) * TB:(j % 2 + 1) * TB]
                             ).then_inc(sQsq.h, 1)
            v.wait_ge(sQn.h, 64)
            vsync(v.reduce_max(sc11[:, 0:1], pn.ap(), axis=AX.X))
            v.tensor_copy(sb_qn2[:, 63:64], sc11[:, 0:1]).then_inc(sQnv.h, 1)
            sQe.n = 64
            sRp.n = 64
            sQsq.n = 64
            sQnv.n = 64

    return [sQw[0], sQw[1], sQnv, sQn, sQsq, sG, sW[0], sW[1], sCsq, sQp,
            sQe, sRot, sRp, sVc]



def _phase2(env):
    nc, stk, geo = env["nc"], env["pstk"], env["geo"]
    gstk = env["gstk"]
    NU, FC = geo["NU"], geo["FC"]
    IDB = env["IDB"]
    sb_kn2, sb_qn2 = env["sb_kn2"], env["sb_qn2"]
    sb_identf, sb_onesr, sb_ixn = env["sb_identf"], env["sb_onesr"], env["sb_ixn"]

    sb_mask = stk.enter_context(nc.sbuf_tensor("p2_mask", [128, 4 * TB],
                                               BF16)).ap()
    sb_sel = stk.enter_context(nc.sbuf_tensor("p2_sel", [128, NU * 2],
                                              F32)).ap()
    sb_lsel = stk.enter_context(nc.sbuf_tensor("p2_lsel", [128, NU * GQ * 8],
                                               BF16)).ap()
    sb_ik = stk.enter_context(nc.sbuf_tensor("p2_ik", [128, HKV * NU],
                                             I32)).ap()
    sb_iv = stk.enter_context(nc.sbuf_tensor("p2_iv", [128, HKV * NU * 4],
                                             I32)).ap()
    sb_iq = stk.enter_context(nc.sbuf_tensor("p2_iq", [128, HKV * GQ * NU],
                                             I32)).ap()

    kslot = stk.enter_context(nc.sbuf_tensor("p2_ks", [128, 2 * TB], BF16)).ap()
    vslot = stk.enter_context(nc.sbuf_tensor("p2_vs", [128, 2 * TB], BF16)).ap()
    qslot = stk.enter_context(nc.sbuf_tensor("p2_qs", [128, 2 * GQ * TB], BF16)).ap()
    pbuf = stk.enter_context(nc.sbuf_tensor("p2_pT", [128, 4 * TB], BF16)).ap()
    cacc = stk.enter_context(nc.sbuf_tensor("p2_cacc", [128, 8 * TB], F32)).ap()
    linv = stk.enter_context(nc.sbuf_tensor("p2_linv", [128, TB], F32)).ap()
    cbias = stk.enter_context(nc.sbuf_tensor("p2_cb", [128, 8], F32)).ap()
    cbd = stk.enter_context(nc.sbuf_tensor("p2_cbd", [128, GQ], F32)).ap()
    cbu = stk.enter_context(nc.sbuf_tensor("p2_cbu", [128, 8], F32)).ap()
    sc11 = stk.enter_context(nc.sbuf_tensor("p2_sc11", [1, 16], F32)).ap()
    scb = stk.enter_context(nc.sbuf_tensor("p2_scb", [1, 16], BF16)).ap()
    sb_lbc = stk.enter_context(nc.sbuf_tensor("p2_lbc", [128, 8 * 128],
                                              F32)).ap()
    ctstgL = stk.enter_context(nc.sbuf_tensor("p2_ctst", [128, 2 * TB],
                                              BF16)).ap()

    xno = stk.enter_context(nc.sbuf_tensor("p2_xno", [128, 2 * EC * TB], BF16)).ap()
    gch = stk.enter_context(nc.sbuf_tensor("p2_g", [128, 2 * EC * 128], BF16)).ap()
    uch = stk.enter_context(nc.sbuf_tensor("p2_u", [128, 2 * EC * 128], BF16)).ap()
    gsb = stk.enter_context(nc.sbuf_tensor("p2_gs", [128, FC], F32)).ap()
    usb = stk.enter_context(nc.sbuf_tensor("p2_us", [128, FC], F32)).ap()
    sg = stk.enter_context(nc.sbuf_tensor("p2_sg", [128, 2 * TB], F32)).ap()
    hst = stk.enter_context(nc.sbuf_tensor("p2_h", [128, 2 * TB], BF16)).ap()

    psT = [stk.enter_context(nc.psum_tensor(f"p2_psT{i}", [128, TB], F32))
           for i in range(2)]
    pctx = [stk.enter_context(nc.psum_tensor(f"p2_pctx{i}", [128, TB], F32))
            for i in range(2)]
    pl = stk.enter_context(nc.psum_tensor("p2_pl", [128, TB], F32))
    pbc = stk.enter_context(nc.psum_tensor("p2_pbc", [128, 1], F32))
    pg = stk.enter_context(nc.psum_tensor("p2_pg", [128, TB], F32))
    pu = stk.enter_context(nc.psum_tensor("p2_pu", [128, TB], F32))

    sTb = S(gstk, nc, "p2_sTb")
    sGx = S(gstk, nc, "p2_sGx")
    sG = [S(gstk, nc, "p2_sG0"), S(gstk, nc, "p2_sG1")]
    sSc = S(gstk, nc, "p2_sSc")
    sBi = S(gstk, nc, "p2_sBi")
    sBb = S(gstk, nc, "p2_sBb")
    sBv = S(gstk, nc, "p2_sBv")
    sBd = S(gstk, nc, "p2_sBd")
    sCbu = S(gstk, nc, "p2_sCbu")
    sSt = S(gstk, nc, "p2_sSt")
    sEx = S(gstk, nc, "p2_sEx")
    sMk = S(gstk, nc, "p2_sMk")
    sLm = S(gstk, nc, "p2_sLm")
    sAc = S(gstk, nc, "p2_sAc")
    sMs = S(gstk, nc, "p2_sMs")
    sLf = S(gstk, nc, "p2_sLf")
    sBc = S(gstk, nc, "p2_sBc")
    sFin = S(gstk, nc, "p2_sFin")
    sCw = [S(gstk, nc, "p2_sCw0"), S(gstk, nc, "p2_sCw1")]
    sW = [S(gstk, nc, "p2_sW0"), S(gstk, nc, "p2_sW1")]
    sPp = S(gstk, nc, "p2_sPp")
    sSi = S(gstk, nc, "p2_sSi")
    sHv = S(gstk, nc, "p2_sHv")
    sHw = [S(gstk, nc, "p2_sHw0"), S(gstk, nc, "p2_sHw1")]

    sched = geo["sched"]
    hw = env["h_scr"]

    NSLOT = HKV * NU
    NH = 2 * FC
    halves_by_slot = [[] for _ in range(NSLOT)]
    for hx in range(NH):
        halves_by_slot[min(NSLOT - 1, hx * NSLOT // NH)].append(hx)
    loads_by_slot = [[] for _ in range(NSLOT)]
    for f in range(FC):
        loads_by_slot[max(0, (2 * f) * NSLOT // NH - 2)].append(f)

    with nc.Block() as blk:
        @blk.sync
        def _(sync):
            for dst, srcd in ((sb_mask, env["maskT_d"]),
                              (sb_sel, env["sel_d"]),
                              (sb_lsel, env["lsel_d"]),
                              (sb_ik, env["ik_d"]),
                              (sb_iv, env["iv_d"]),
                              (sb_iq, env["iq_d"]),
                              (sb_lbc[0:8, :], env["lbc_d"])):
                sync.dma_start(out=dst[:, :] if dst.shape[0] == 128 else dst,
                               in_=srcd).then_inc(sTb.h, 16)
            sTb.n = 7 * 16
            sync.dma_start(out=gsb[:, :], in_=env["gate_s"]).then_inc(sSc.h, 16)
            sync.dma_start(out=usb[:, :], in_=env["up_s"]).then_inc(sSc.h, 16)
            sSc.n = 32
            for h in range(HKV):
                for u in range(NU):
                    uu = h * NU + u
                    for hx in halves_by_slot[uu]:
                        f, th = divmod(hx, 2)
                        sync.wait_ge(sHv.h, hx + 1)
                        sync.dma_start(out=hw[f * 128:(f + 1) * 128,
                                              th * TB:(th + 1) * TB],
                                       in_=hst[:, th * TB:(th + 1) * TB]
                                       ).then_inc(sHw[th].h, 16)
                ctxv = env["ctx_scr"].rearrange("(t q p) c -> q p t c",
                                                t=8, p=128)
                ctsv = ctstgL.rearrange("p (sl s c) -> p sl s c", sl=2, c=128)
                for acix in range(8):
                    qq, a = divmod(acix, 2)
                    qh = h * GQ + qq
                    sync.wait_ge(sFin.h, 8 * h + acix + 1)
                    sync.dma_start(out=ctxv[qh][:, a * 4:a * 4 + 4],
                                   in_=ctsv[:, acix % 2]
                                   ).then_inc(sCw[acix % 2].h, 16)
            sHw[0].n = FC * 16
            sHw[1].n = FC * 16
            sCw[0].n = HKV * 4 * 16
            sCw[1].n = HKV * 4 * 16

        @blk.gpsimd
        def _(gp):
            gp.wait_ge(sTb.h, 7 * 16)
            for col in range(2 * EC):
                gp.indirect_dma_start(
                    xno[:, col * TB:(col + 1) * TB], None,
                    env["xn_scr"],
                    bass.IndirectOffsetOnAxis(sb_ixn[:, col:col + 1], 0)
                ).then_inc(sGx.h, 16)
            sGx.n = 2 * EC * 16
            for h in range(HKV):
                for u in range(NU):
                    uu = h * NU + u
                    if uu >= 2:
                        gp.wait_ge(sLm.h, 4 * GQ * (uu - 1))
                    par = uu % 2
                    gp.indirect_dma_start(
                        kslot[:, par * TB:(par + 1) * TB], None, env["k_scr"],
                        bass.IndirectOffsetOnAxis(
                            sb_ik[:, uu:uu + 1], 0)).then_inc(sG[par].h, 16)
                    for s in range(4):
                        gp.indirect_dma_start(
                            vslot[:, par * TB + s * 128:par * TB + (s + 1) * 128],
                            None,
                            env["v_scr"].rearrange("r (h d) -> (r h) d", d=128),
                            bass.IndirectOffsetOnAxis(
                                sb_iv[:, uu * 4 + s:uu * 4 + s + 1], 0)
                        ).then_inc(sG[par].h, 16)
                    for qq in range(GQ):
                        gp.indirect_dma_start(
                            qslot[:, (par * GQ + qq) * TB:(par * GQ + qq + 1) * TB],
                            None, env["q_scr"],
                            bass.IndirectOffsetOnAxis(
                                sb_iq[:, uu * GQ + qq:uu * GQ + qq + 1], 0)
                        ).then_inc(sG[par].h, 16)
                    for f in loads_by_slot[uu]:
                        if f >= 2:
                            gp.wait_ge(sPp.h, 4 * (f - 1))
                        gp.dma_start(
                            out=gch.rearrange("p (c ec f) -> p c ec f", c=2,
                                              f=128)[:, f % 2],
                            in_=env["gateTr"][f]).then_inc(sW[f % 2].h, 16)
                        gp.dma_start(
                            out=uch.rearrange("p (c ec f) -> p c ec f", c=2,
                                              f=128)[:, f % 2],
                            in_=env["upTr"][f]).then_inc(sW[f % 2].h, 16)
            NUU = HKV * NU
            sG[0].n = (NUU + 1) // 2 * 9 * 16
            sG[1].n = NUU // 2 * 9 * 16
            sW[0].n = (FC + 1) // 2 * 32
            sW[1].n = FC // 2 * 32

        @blk.scalar
        def _(sc):
            nBi = nEx = nSi = 0
            sc.wait_ge(sSc.h, 32)
            for h in range(HKV):
                for qq in range(GQ):
                    for a in range(2):
                        j = h * 8 + qq * 2 + a
                        sc.wait_ge(sBv.h, 4 * j + 1)
                        sc.activation(sc11[:, 3:4], sc11[:, 2:3],
                                      AF.Ln).then_inc(sBi.h, 1)
                        sc.wait_ge(sBv.h, 4 * j + 2)
                        sc.activation(sc11[:, 4:5], sc11[:, 3:4],
                                      AF.Exp).then_inc(sBi.h, 1)
                        nBi += 2
                for u in range(NU):
                    uu = h * NU + u
                    for qq in range(GQ):
                        kq = uu * GQ + qq
                        sc.wait_ge(sCbu.h, kq + 1)
                        for s in range(4):
                            k = kq * 4 + s
                            sc.wait_ge(sSt.h, k + 1)
                            if kq >= 1:
                                sc.wait_ge(sLm.h, 4 * kq)
                            sc.activation(pbuf[:, s * TB:(s + 1) * TB],
                                          psT[k % 2].ap(), AF.Exp,
                                          bias=cbu[:, kq % 8:kq % 8 + 1]
                                          ).then_inc(sEx.h, 1)
                            nEx += 1
                    for hx in halves_by_slot[uu]:
                        f, th = divmod(hx, 2)
                        sc.wait_ge(sPp.h, 4 * f + 2 * (th + 1))
                        if hx >= 2:
                            sc.wait_ge(sHv.h, hx - 1)
                        sc.activation(sg[:, th * TB:(th + 1) * TB], pg.ap(),
                                      AF.Silu, scale=gsb[:, f:f + 1]
                                      ).then_inc(sSi.h, 1)
                        nSi += 1
            sBi.n = nBi
            sEx.n = nEx
            sSi.n = nSi

        @blk.tensor
        def _(t):
            t.wait_ge(sTb.h, 7 * 16)
            for h in range(HKV):
                for qq in range(GQ):
                    for a in range(2):
                        j = h * 8 + qq * 2 + a
                        t.wait_ge(sBv.h, 4 * j + 3)
                        if j >= 1:
                            t.wait_ge(sBv.h, 4 * j)
                        t.matmul(pbc.ap(), sb_onesr[:, :], scb[:, 0:1],
                                 start=True, stop=True).then_inc(sBb.h, 1)
                for u in range(NU):
                    uu = h * NU + u
                    par = uu % 2
                    t.wait_ge(sG[par].h, 16 * 9 * (uu // 2 + 1))
                    t.wait_ge(sMs.h, 8 * (h + 1))
                    if u == 0 and h > 0:
                        t.wait_ge(sFin.h, 8 * h)
                        t.wait_ge(sLf.h, h)
                    for qq in range(GQ):
                        kq = uu * GQ + qq
                        for s in range(4):
                            k = kq * 4 + s
                            if k >= 2:
                                t.wait_ge(sEx.h, k - 1)
                            t.matmul(psT[k % 2].ap(),
                                     kslot[:, par * TB + s * 128:
                                           par * TB + (s + 1) * 128],
                                     qslot[:, (par * GQ + qq) * TB:
                                           (par * GQ + qq + 1) * TB],
                                     start=True, stop=True).then_inc(sSt.h, 1)
                        for s in range(4):
                            k = kq * 4 + s
                            if u < 2:
                                t.wait_ge(sMk.h, (h * 2 * GQ + u * GQ + qq) * 4
                                          + s + 1)
                            else:
                                t.wait_ge(sEx.h, k + 1)
                            if s == 0 and kq >= 2:
                                t.wait_ge(sAc.h, 2 * (kq - 1))
                            t.matmul(pctx[kq % 2].ap(),
                                     vslot[:, par * TB + s * 128:
                                           par * TB + (s + 1) * 128],
                                     pbuf[:, s * TB:(s + 1) * TB],
                                     start=(s == 0), stop=(s == 3))
                            t.matmul(pl.ap()[0:8, :],
                                     sb_lsel[:, (u * GQ + qq) * 8:
                                             (u * GQ + qq + 1) * 8],
                                     pbuf[:, s * TB:(s + 1) * TB],
                                     start=(u == 0 and qq == 0 and s == 0),
                                     stop=(u == NU - 1 and qq == GQ - 1
                                           and s == 3)).then_inc(sLm.h, 1)
                    for hx in halves_by_slot[uu]:
                        f, th = divmod(hx, 2)
                        if hx == 0:
                            t.wait_ge(sGx.h, 2 * EC * 16)
                        if th == 0:
                            t.wait_ge(sW[f % 2].h, 32 * (f // 2 + 1))
                        if hx >= 1:
                            t.wait_ge(sSi.h, hx)
                            t.wait_ge(sHv.h, hx)
                        for ec in range(EC):
                            gl = gch[:, (f % 2) * EC * 128 + ec * 128:
                                     (f % 2) * EC * 128 + (ec + 1) * 128]
                            ul = uch[:, (f % 2) * EC * 128 + ec * 128:
                                     (f % 2) * EC * 128 + (ec + 1) * 128]
                            rh = xno[:, (th * EC + ec) * TB:
                                     (th * EC + ec + 1) * TB]
                            t.matmul(pg.ap(), gl, rh, start=(ec == 0),
                                     stop=(ec == EC - 1))
                            i2 = t.matmul(pu.ap(), ul, rh, start=(ec == 0),
                                          stop=(ec == EC - 1))
                            if ec == EC - 1:
                                i2.then_inc(sPp.h, 2)
                t.wait_ge(sLf.h, h + 1)
                t.wait_ge(sAc.h, 2 * GQ * NU * (h + 1))
                for acix in range(8):
                    if 8 * h + acix >= 2:
                        t.wait_ge(sFin.h, 8 * h + acix - 1)
                    t.matmul(psT[acix % 2].ap(),
                             sb_lbc[0:8, acix * 128:(acix + 1) * 128],
                             linv[0:8, 0:TB], start=True,
                             stop=True).then_inc(sBc.h, 1)
            sBb.n = HKV * GQ * 2
            sSt.n = HKV * NU * GQ * 4
            sLm.n = HKV * NU * GQ * 4
            sPp.n = FC * 4
            sBc.n = HKV * 8

        @blk.vector
        def _(v):
            nBv = nMk = nMs = nHv = nBd = nCbu = 0
            nCw = [0, 0]
            v.wait_ge(sTb.h, 7 * 16)
            v.wait_ge(sSc.h, 32)
            for h in range(HKV):
                for z in range(8):
                    v.memset(cacc[:, z * TB:(z + 1) * TB],
                             0.0).then_inc(sMs.h, 1)
                    nMs += 1
                for qq in range(GQ):
                    for a in range(2):
                        j = h * 8 + qq * 2 + a
                        if j >= 1:
                            v.wait_ge(sBi.h, 2 * j - 1)
                        qh = h * GQ + qq
                        v.tensor_mul(sc11[:, 2:3],
                                     sb_qn2[:, qh * 2 + a:qh * 2 + a + 1],
                                     sb_kn2[:, h:h + 1]).then_inc(sBv.h, 1)
                        v.wait_ge(sBi.h, 2 * j + 1)
                        v.tensor_scalar_mul(sc11[:, 3:4], sc11[:, 3:4],
                                            0.5).then_inc(sBv.h, 1)
                        v.wait_ge(sBi.h, 2 * j + 2)
                        if j >= 1:
                            v.wait_ge(sBb.h, j)
                        v.tensor_scalar_mul(scb[:, 0:1], sc11[:, 4:5],
                                            -1.0).then_inc(sBv.h, 1)
                        v.wait_ge(sBb.h, j + 1)
                        v.tensor_copy(cbias[:, (qq * 2 + a):(qq * 2 + a) + 1],
                                      pbc.ap()).then_inc(sBv.h, 1)
                        nBv += 4
                v.wait_ge(sBv.h, nBv)
                for qq in range(GQ):
                    v.tensor_sub(cbd[:, qq:qq + 1], cbias[:, qq * 2:qq * 2 + 1],
                                 cbias[:, qq * 2 + 1:qq * 2 + 2]
                                 ).then_inc(sBd.h, 1)
                    nBd += 1
                v.wait_ge(sBd.h, nBd)
                for u in range(NU):
                    uu = h * NU + u
                    for qq in range(GQ):
                        kq = uu * GQ + qq
                        if kq >= 8:
                            v.wait_ge(sEx.h, 4 * (kq - 7))
                        v.tensor_scalar(cbu[:, kq % 8:kq % 8 + 1],
                                        cbd[:, qq:qq + 1],
                                        sb_sel[:, u * 2:u * 2 + 1],
                                        cbias[:, qq * 2 + 1:qq * 2 + 2],
                                        ALU.mult, ALU.add).then_inc(sCbu.h, 1)
                        nCbu += 1
                    for qq in range(GQ):
                        kq = uu * GQ + qq
                        if u < 2:
                            for s in range(4):
                                k = kq * 4 + s
                                v.wait_ge(sEx.h, k + 1)
                                v.tensor_mul(pbuf[:, s * TB:(s + 1) * TB],
                                             pbuf[:, s * TB:(s + 1) * TB],
                                             sb_mask[:, s * TB:(s + 1) * TB]
                                             ).then_inc(sMk.h, 1)
                                nMk += 1
                        v.wait_ge(sLm.h, 4 * (kq + 1))
                        nAc0 = 2 * kq
                        for a in range(2):
                            acix = qq * 2 + a
                            selap = sb_sel[:, u * 2 + a:u * 2 + a + 1]
                            v.scalar_tensor_tensor(
                                cacc[:, acix * TB:(acix + 1) * TB],
                                pctx[kq % 2].ap(), selap,
                                cacc[:, acix * TB:(acix + 1) * TB],
                                ALU.mult, ALU.add).then_inc(sAc.h, 1)
                        v.wait_ge(sAc.h, nAc0 + 2)
                    for hx in halves_by_slot[uu]:
                        f, th = divmod(hx, 2)
                        v.wait_ge(sSi.h, hx + 1)
                        if f >= 1:
                            v.wait_ge(sHw[th].h, 16 * f)
                        v.scalar_tensor_tensor(
                            hst[:, th * TB:(th + 1) * TB],
                            pu.ap(), usb[:, f:f + 1],
                            sg[:, th * TB:(th + 1) * TB],
                            ALU.mult, ALU.mult).then_inc(sHv.h, 1)
                        nHv += 1
                v.wait_ge(sLm.h, NU * GQ * 4 * (h + 1))
                v.reciprocal(linv[0:8, 0:TB],
                             pl.ap()[0:8, :]).then_inc(sLf.h, 1)
                for acix in range(8):
                    v.wait_ge(sBc.h, 8 * h + acix + 1)
                    p = acix % 2
                    if nCw[p] >= 1:
                        v.wait_ge(sCw[p].h, 16 * nCw[p])
                    v.tensor_mul(ctstgL[:, p * TB:(p + 1) * TB],
                                 cacc[:, acix * TB:(acix + 1) * TB],
                                 psT[p].ap()).then_inc(sFin.h, 1)
                    nCw[p] += 1
            sBv.n = nBv
            sBd.n = nBd
            sCbu.n = nCbu
            sMk.n = nMk
            sAc.n = HKV * NU * GQ * 2
            sMs.n = nMs
            sLf.n = HKV
            sFin.n = HKV * 8
            sHv.n = nHv

    return [sFin, sCw[0], sCw[1], sHw[0], sHw[1], sG[0], sG[1], sTb, sGx,
            sSc, sBi, sBb, sBv, sBd, sCbu, sSt, sEx, sMk, sLm, sAc, sMs,
            sLf, sBc, sW[0], sW[1], sPp, sSi, sHv]



def _phase3a(env):
    nc, stk, geo = env["nc"], env["pstk"], env["geo"]
    gstk = env["gstk"]

    woc = stk.enter_context(nc.sbuf_tensor("p3a_wo", [128, 2 * EC * TB], BF16)).ap()
    ctb = stk.enter_context(nc.sbuf_tensor("p3a_ct", [128, 2 * EC * 128],
                                           BF16)).ap()
    xot = stk.enter_context(nc.sbuf_tensor("p3a_xo", [128, 2 * TB], F32)).ap()
    ot = stk.enter_context(nc.sbuf_tensor("p3a_ot", [128, 2 * TB], F32)).ap()
    po = [stk.enter_context(nc.psum_tensor(f"p3a_po{i}", [128, TB], F32))
          for i in range(2)]

    sW = [S(gstk, nc, "p3a_sW0"), S(gstk, nc, "p3a_sW1")]
    sXo = [S(gstk, nc, "p3a_sXo0"), S(gstk, nc, "p3a_sXo1")]
    sCb = [S(gstk, nc, "p3a_sCb0"), S(gstk, nc, "p3a_sCb1")]
    sPo = S(gstk, nc, "p3a_sPo")
    sEv = S(gstk, nc, "p3a_sEv")
    sOw = [S(gstk, nc, "p3a_sOw0"), S(gstk, nc, "p3a_sOw1")]
    env["p3a_sOw"] = sOw[0]

    ctxv = env["ctx_scr"].rearrange("(t hd p) c -> t p hd c", t=8, p=128)
    ctbv = ctb.rearrange("p (sl hd c) -> p sl hd c", sl=2, c=128)

    with nc.Block() as blk:
        @blk.gpsimd
        def _(gp):
            for ew in range(8):
                if ew >= 2:
                    gp.wait_ge(sPo.h, 8 * (ew - 1))
                gp.dma_start(
                    out=woc.rearrange("p (c ec d) -> p c ec d", c=2,
                                      d=TB)[:, ew % 2],
                    in_=env["woTr"][ew]).then_inc(sW[ew % 2].h, 16)
            sW[0].n = 4 * 16
            sW[1].n = 4 * 16

        @blk.sync
        def _(sync):
            for ew in range(8):
                for tt in range(8):
                    j = ew * 8 + tt
                    if j >= 2:
                        sync.wait_ge(sEv.h, j - 1)
                    sync.dma_start(
                        out=xot[:, (j % 2) * TB:(j % 2 + 1) * TB],
                        in_=env["x_own"][tt * 128:(tt + 1) * 128,
                                         ew * TB:(ew + 1) * TB]
                    ).then_inc(sXo[j % 2].h, 16)
                    if j >= 2:
                        sync.wait_ge(sPo.h, j - 1)
                    sync.dma_start(
                        out=ctbv[:, j % 2],
                        in_=ctxv[tt]
                    ).then_inc(sCb[j % 2].h, 16)
            sXo[0].n = 32 * 16
            sXo[1].n = 32 * 16
            sCb[0].n = 32 * 16
            sCb[1].n = 32 * 16

        @blk.tensor
        def _(t):
            for ew in range(8):
                t.wait_ge(sW[ew % 2].h, 16 * (ew // 2 + 1))
                for tt in range(8):
                    j = ew * 8 + tt
                    pb = po[j % 2]
                    t.wait_ge(sCb[j % 2].h, 16 * (j // 2 + 1))
                    if j >= 2:
                        t.wait_ge(sEv.h, j - 1)
                    for hdc in range(EC):
                        ins = t.matmul(
                            pb.ap(),
                            ctb[:, (j % 2) * EC * 128 + hdc * 128:
                                (j % 2) * EC * 128 + (hdc + 1) * 128],
                            woc[:, (ew % 2) * EC * TB + hdc * TB:
                                (ew % 2) * EC * TB + (hdc + 1) * TB],
                            start=(hdc == 0), stop=(hdc == EC - 1))
                    ins.then_inc(sPo.h, 1)
            sPo.n = 64

        @blk.vector
        def _(v):
            for ew in range(8):
                for tt in range(8):
                    j = ew * 8 + tt
                    v.wait_ge(sPo.h, j + 1)
                    v.wait_ge(sXo[j % 2].h, 16 * (j // 2 + 1))
                    if j >= 2:
                        v.wait_ge(sOw[j % 2].h, 16 * (j // 2))
                    v.tensor_add(ot[:, (j % 2) * TB:(j % 2 + 1) * TB],
                                 po[j % 2].ap(),
                                 xot[:, (j % 2) * TB:(j % 2 + 1) * TB]
                                 ).then_inc(sEv.h, 1)
            sEv.n = 64

        @blk.scalar
        def _(sc):
            for ew in range(8):
                for tt in range(8):
                    j = ew * 8 + tt
                    sc.wait_ge(sEv.h, j + 1)
                    sc.dma_start(out=env["out"][tt * 128:(tt + 1) * 128,
                                                ew * TB:(ew + 1) * TB],
                                 in_=ot[:, (j % 2) * TB:(j % 2 + 1) * TB]
                                 ).then_inc(sOw[j % 2].h, 16)
            sOw[0].n = 32 * 16
            sOw[1].n = 32 * 16

    return [sOw[0], sOw[1], sW[0], sW[1], sXo[0], sXo[1], sCb[0],
            sCb[1], sPo, sEv]



def _phase3c(env):
    nc, stk, geo = env["nc"], env["pstk"], env["geo"]
    gstk = env["gstk"]
    FC = geo["FC"]
    groups = []
    fc0 = 0
    while fc0 < FC:
        gsz = min(8, FC - fc0)
        groups.append((fc0, gsz))
        fc0 += gsz
    NGR = len(groups)
    EH = E // 2

    dch = stk.enter_context(nc.sbuf_tensor("p3c_d", [128, 8 * E], BF16)).ap()
    hch = stk.enter_context(nc.sbuf_tensor("p3c_h", [128, 8 * 2 * TB], BF16)).ap()
    dss = stk.enter_context(nc.sbuf_tensor("p3c_ds", [128, E], F32)).ap()
    od = stk.enter_context(nc.sbuf_tensor("p3c_od", [128, 2 * E], F32)).ap()
    pdh = [stk.enter_context(nc.psum_tensor(f"p3c_pd{i}", [128, EH], F32))
           for i in range(2)]

    sW = S(gstk, nc, "p3c_sW")
    sH = S(gstk, nc, "p3c_sH")
    sDs = S(gstk, nc, "p3c_sDs")
    sPd = S(gstk, nc, "p3c_sPd")
    sEv = S(gstk, nc, "p3c_sEv")
    sAo = [S(gstk, nc, "p3c_sAo0"), S(gstk, nc, "p3c_sAo1")]
    env["p3c_sAo"] = sAo[0]

    with nc.Block() as blk:
        @blk.sync
        def _(sync):
            sync.dma_start(out=dss[:, :], in_=env["down_sb"]).then_inc(sDs.h, 16)
            sDs.n = 16
            for gi, (fc0_, gsz) in enumerate(groups):
                if gi >= 1:
                    sync.wait_ge(sPd.h, 16 * gi)
                for j in range(gsz):
                    fc = fc0_ + j
                    sync.dma_start(out=hch[:, j * 2 * TB:(j + 1) * 2 * TB],
                                   in_=env["h_scr"][fc * 128:(fc + 1) * 128, :]
                                   ).then_inc(sH.h, 16)
            sH.n = sum(g for _, g in groups) * 16

        @blk.gpsimd
        def _(gp):
            for gi, (fc0_, gsz) in enumerate(groups):
                if gi >= 1:
                    gp.wait_ge(sPd.h, 16 * gi)
                for j in range(gsz):
                    fc = fc0_ + j
                    gp.dma_start(out=dch[:, j * E:(j + 1) * E],
                                 in_=env["downT"][fc * 128:(fc + 1) * 128, :]
                                 ).then_inc(sW.h, 16)
                for tt in range(8):
                    j = gi * 8 + tt
                    for eh in range(2):
                        m = j * 2 + eh
                        gp.wait_ge(sEv.h, m + 1)
                        for cq in range(2):
                            gp.dma_start(
                                out=env["out"][tt * 128:(tt + 1) * 128,
                                               eh * EH + cq * 1024:
                                               eh * EH + (cq + 1) * 1024],
                                in_=od[:, (j % 2) * E + eh * EH + cq * 1024:
                                       (j % 2) * E + eh * EH + (cq + 1) * 1024],
                                accum_op=ALU.add).then_inc(sAo[j % 2].h, 16)
            sW.n = sum(g for _, g in groups) * 16
            sAo[0].n = (NGR * 8 + 1) // 2 * 64
            sAo[1].n = NGR * 8 // 2 * 64

        @blk.tensor
        def _(t):
            hcnt = 0
            for gi, (fc0_, gsz) in enumerate(groups):
                hcnt += gsz
                t.wait_ge(sW.h, 16 * hcnt)
                t.wait_ge(sH.h, 16 * hcnt)
                for tt in range(8):
                    j = gi * 8 + tt
                    for eh in range(2):
                        m = j * 2 + eh
                        if m >= 2:
                            t.wait_ge(sEv.h, m - 1)
                        for k in range(gsz):
                            for e4 in range(4):
                                ins = t.matmul(
                                    pdh[eh].ap()[:, e4 * TB:(e4 + 1) * TB],
                                    hch[:, k * 2 * TB + tt * 128:
                                        k * 2 * TB + (tt + 1) * 128],
                                    dch[:, k * E + eh * EH + e4 * TB:
                                        k * E + eh * EH + (e4 + 1) * TB],
                                    start=(k == 0), stop=(k == gsz - 1))
                        ins.then_inc(sPd.h, 1)
            sPd.n = NGR * 16

        @blk.vector
        def _(v):
            v.wait_ge(sDs.h, 16)
            for gi in range(NGR):
                for tt in range(8):
                    j = gi * 8 + tt
                    for eh in range(2):
                        m = j * 2 + eh
                        v.wait_ge(sPd.h, m + 1)
                        if j >= 2:
                            v.wait_ge(sAo[j % 2].h,
                                      64 * (j // 2 - 1) + 32 * (eh + 1))
                        v.tensor_mul(od[:, (j % 2) * E + eh * EH:
                                        (j % 2) * E + (eh + 1) * EH],
                                     pdh[eh].ap(),
                                     dss[:, eh * EH:(eh + 1) * EH]
                                     ).then_inc(sEv.h, 1)
            sEv.n = NGR * 16

    return [sAo[0], sAo[1], sW, sH, sDs, sPd, sEv]



def kernel(L0_attn_x, L0_attn_Wq, L0_attn_Wk, L0_attn_Wv, L0_attn_Wo,
           L0_ffn_gate_w, L0_ffn_gate_s, L0_ffn_up_w, L0_ffn_up_s,
           L0_ffn_down_w, L0_ffn_down_s):
    geo0 = full_geo(0)
    sh = host_shared(geo0, np.asarray(L0_attn_x, np.float32),
                     np.asarray(L0_attn_Wq, np.float32),
                     np.asarray(L0_attn_Wk, np.float32),
                     np.asarray(L0_attn_Wv, np.float32),
                     np.asarray(L0_attn_Wo, np.float32),
                     np.asarray(L0_ffn_gate_w, np.int32),
                     np.asarray(L0_ffn_up_w, np.int32),
                     np.asarray(L0_ffn_down_w, np.int32),
                     np.asarray(L0_ffn_gate_s, np.float32),
                     np.asarray(L0_ffn_up_s, np.float32),
                     np.asarray(L0_ffn_down_s, np.float32))
    x = sh["x_full"]
    cosf, sinf = sh["cos_k"], sh["sin_k"]
    nc = build_program(geo0)
    in_maps = []
    for c in range(NCORE):
        geo = full_geo(c)
        d = dict(sh)
        d.update(host_percore(geo))
        rows = np.concatenate([np.arange(b * TB, (b + 1) * TB)
                               for b in geo["own"]])
        d["x_own"] = np.ascontiguousarray(x[rows])
        d["cos_q"] = np.ascontiguousarray(cosf[:, rows] * np.float32(ISQD))
        d["sin_q"] = np.ascontiguousarray(sinf[:, rows] * np.float32(ISQD))
        in_maps.append(d)
    res = run_bass_kernel_spmd(nc, in_maps, core_ids=list(range(NCORE)))
    outp = np.zeros((1, geo0["T"], E), np.float32)
    for c in range(NCORE):
        geo = full_geo(c)
        o = res.results[c]["out"]
        for a, b in enumerate(geo["own"]):
            outp[0, b * TB:(b + 1) * TB] = o[a * TB:(a + 1) * TB]
    return outp


# version 07
# speedup vs baseline: 1.1288x; 1.1288x over previous
import contextlib
import math
import numpy as np
import ml_dtypes
import concourse.bass as bass
import concourse.mybir as mybir
from concourse.bass_utils import run_bass_kernel_spmd

F32 = mybir.dt.float32
BF16 = mybir.dt.bfloat16
I32 = mybir.dt.int32
AF = mybir.ActivationFunctionType
ALU = mybir.AluOpType
AX = mybir.AxisListType
BF = ml_dtypes.bfloat16

E = 4096
DH = 128
HKV = 8
HQ = 32
GQ = 4
EC = 32
TB = 512
NCORE = 8
EPS = 1e-5
ISQD = 1.0 / math.sqrt(DH)


def full_geo(core):
    NB = 16
    own = [core, NB - 1 - core]
    A, B = own
    sched = [(0, A), (1, B)] + [(0, k) for k in range(A)] + [(1, k) for k in range(B)]
    return dict(NB=NB, T=NB * TB, own=own, sched=sched, NU=len(sched),
                DFF=11008, FC=86)



def _rope_tables(T):
    dt = np.float32
    theta, factor, hi_f, lo_f, octx = (dt(500000.0), dt(32.0), dt(4.0), dt(1.0),
                                       dt(8192.0))
    inv = (dt(1.0) / (theta ** (np.arange(0, DH, 2, dtype=dt) / dt(DH)))).astype(dt)
    wav = (dt(2.0) * dt(math.pi) / inv).astype(dt)
    inv = np.where(wav > octx / lo_f, inv / factor, inv).astype(dt)
    smooth = ((octx / wav - lo_f) / (hi_f - lo_f)).astype(dt)
    smoothed = ((dt(1.0) - smooth) * inv / factor + smooth * inv).astype(dt)
    is_med = (~(wav < octx / lo_f)) & (~(wav > octx / hi_f))
    inv = np.where(is_med, smoothed, inv).astype(dt)
    pos = np.arange(T, dtype=dt)
    freqs = pos[:, None] * inv[None, :]
    emb = np.concatenate([freqs, freqs], axis=-1)
    return np.cos(emb).astype(dt), np.sin(emb).astype(dt)


def host_shared(geo, x, wq, wk, wv, wo, gw, uw, dw, gs, us, ds):
    T, FC = geo["T"], geo["FC"]
    cos, sin = _rope_tables(T)
    sh = {}
    sh["x_full"] = np.ascontiguousarray(x.reshape(T, E))
    sh["cos_k"] = np.ascontiguousarray(cos.T)
    sh["sin_k"] = np.ascontiguousarray(sin.T)
    sh["wkT"] = np.ascontiguousarray(wk.T)
    sh["wvT"] = np.ascontiguousarray(wv.T)
    sh["wqTr"] = np.ascontiguousarray(wq.reshape(8, 512, EC, 128).transpose(0, 3, 2, 1))
    sh["woTr"] = np.ascontiguousarray(wo.reshape(8, 512, EC, 128).transpose(0, 3, 2, 1))
    sh["gateTr"] = np.ascontiguousarray(
        gw.reshape(FC, 128, EC, 128).transpose(0, 3, 2, 1))
    sh["upTr"] = np.ascontiguousarray(
        uw.reshape(FC, 128, EC, 128).transpose(0, 3, 2, 1))
    sh["downT"] = np.ascontiguousarray(dw.T)
    sh["gate_s"] = np.ascontiguousarray(gs.reshape(FC, 128).T)
    sh["up_s"] = np.ascontiguousarray(us.reshape(FC, 128).T)
    sh["down_sb"] = np.ascontiguousarray(np.broadcast_to(ds.reshape(1, E), (128, E)))
    sh["ident_b"] = np.eye(128, dtype=BF)
    sh["ident_f"] = np.eye(128, dtype=np.float32)
    sh["ones_c"] = np.ones((128, 128), dtype=BF)
    sh["ones_r"] = np.ones((1, 128), dtype=BF)
    P = np.zeros((128, 128), dtype=BF)
    for p in range(64):
        P[p + 64, p] = BF(-1.0)
        P[p, p + 64] = BF(1.0)
    sh["rotmat"] = P
    o = np.arange(4)[:, None, None] * 128 + np.arange(128)[None, :, None]
    i = np.arange(TB)[None, None, :]
    sh["maskT"] = np.ascontiguousarray((o <= i).astype(BF).transpose(1, 0, 2)
                                       .reshape(128, 4 * TB))
    return sh


def host_percore(geo):
    NB, NU, own, sched = geo["NB"], geo["NU"], geo["own"], geo["sched"]
    p128 = np.arange(128, dtype=np.int32)
    ik = np.zeros((128, HKV * NU), np.int32)
    iv = np.zeros((128, HKV * NU * 4), np.int32)
    iq = np.zeros((128, HKV * GQ * NU), np.int32)
    for h in range(HKV):
        for u, (qsel, ksel) in enumerate(sched):
            ik[:, h * NU + u] = (h * NB + ksel) * 128 + p128
            for s in range(4):
                iv[:, (h * NU + u) * 4 + s] = \
                    (((ksel * 4 + s) * 128 + p128) * 8 + h)
            for qq in range(GQ):
                qh = h * GQ + qq
                iq[:, (h * NU + u) * GQ + qq] = (qh * 2 + qsel) * 128 + p128
    ixn = np.zeros((128, 2 * EC), np.int32)
    for a, b in enumerate(own):
        for ec in range(EC):
            ixn[:, a * EC + ec] = (b * EC + ec) * 128 + p128
    sel = np.zeros((128, NU * 2), np.float32)
    for u, (qsel, ksel) in enumerate(sched):
        sel[:, u * 2 + qsel] = 1.0
    return {"ik": ik, "iv": iv, "iq": iq, "ixn": ixn, "sel": sel}



class S:
    def __init__(self, stack, nc, name):
        self.h = stack.enter_context(nc.semaphore(name))
        self.n = 0


def _barrier(nc, terms):
    if not terms:
        return
    with nc.Block() as blk:
        for attr in ("sync", "gpsimd", "scalar", "vector", "tensor"):
            def body(eng, _terms=tuple(terms)):
                for s in _terms:
                    eng.wait_ge(s.h, s.n)
            getattr(blk, attr)(body)



def build_program(geo, phases=("p1", "p1b", "p2", "p3a", "p3b", "p3c"), dbg=False):
    NB, T, NU, FC = geo["NB"], geo["T"], geo["NU"], geo["FC"]
    nc = bass.Bass("TRN2", target_bir_lowering=False,
                   detect_race_conditions=False)

    def din(name, shape, dt):
        return nc.dram_tensor(name, list(shape), dt, kind="ExternalInput").ap()

    env = dict(nc=nc, geo=geo, dbg=dbg)
    env["x_full"] = din("x_full", (T, E), F32)
    env["x_own"] = din("x_own", (2 * TB, E), F32)
    env["cos_k"] = din("cos_k", (DH, T), F32)
    env["sin_k"] = din("sin_k", (DH, T), F32)
    env["cos_q"] = din("cos_q", (DH, 2 * TB), F32)
    env["sin_q"] = din("sin_q", (DH, 2 * TB), F32)
    env["wkT"] = din("wkT", (E, 1024), F32)
    env["wvT"] = din("wvT", (E, 1024), F32)
    env["wqTr"] = din("wqTr", (8, 128, EC, 512), F32)
    env["woTr"] = din("woTr", (8, 128, EC, 512), F32)
    env["gateTr"] = din("gateTr", (FC, 128, EC, 128), I32)
    env["upTr"] = din("upTr", (FC, 128, EC, 128), I32)
    env["downT"] = din("downT", (FC * 128, E), I32)
    env["gate_s"] = din("gate_s", (128, FC), F32)
    env["up_s"] = din("up_s", (128, FC), F32)
    env["down_sb"] = din("down_sb", (128, E), F32)
    ident_b = din("ident_b", (128, 128), BF16)
    ident_f = din("ident_f", (128, 128), F32)
    ones_c = din("ones_c", (128, 128), BF16)
    ones_r = din("ones_r", (1, 128), BF16)
    rotmat = din("rotmat", (128, 128), BF16)
    maskT = din("maskT", (128, 4 * TB), BF16)
    ik_t = din("ik", (128, HKV * NU), I32)
    iv_t = din("iv", (128, HKV * NU * 4), I32)
    iq_t = din("iq", (128, HKV * GQ * NU), I32)
    ixn_t = din("ixn", (128, 2 * EC), I32)
    sel_t = din("sel", (128, NU * 2), F32)

    env["out"] = nc.dram_tensor("out", [2 * TB, E], F32, kind="ExternalOutput").ap()

    skind = "ExternalOutput" if dbg else "Internal"
    env["k_scr"] = nc.dram_tensor("k_scr", [HKV * NB * 128, TB], BF16, kind=skind).ap()
    env["v_scr"] = nc.dram_tensor("v_scr", [NB * 4 * 128, 1024], BF16, kind=skind).ap()
    env["q_scr"] = nc.dram_tensor("q_scr", [HQ * 2 * 128, TB], BF16, kind=skind).ap()
    env["xn_scr"] = nc.dram_tensor("xn_scr", [NB * EC * 128, TB], BF16,
                                   kind=skind).ap()
    env["h_scr"] = nc.dram_tensor("h_scr", [FC * 128, 2 * TB], BF16, kind=skind).ap()
    env["ctx_scr"] = (nc.dram_tensor("ctx_scr", [128, HQ * 2 * TB], BF16,
                                     kind="ExternalOutput").ap() if dbg else None)

    sb_const = nc.alloc_sbuf_tensor("sb_const", [128, 128 * 3], BF16)
    env["sb_identf"] = nc.alloc_sbuf_tensor("sb_identf", [128, 128], F32).ap()
    env["sb_onesr"] = nc.alloc_sbuf_tensor("sb_onesr", [1, 128], BF16).ap()
    env["sb_mask"] = nc.alloc_sbuf_tensor("sb_mask", [128, 4 * TB], BF16).ap()
    env["sb_ik"] = nc.alloc_sbuf_tensor("sb_ik", [128, HKV * NU], I32).ap()
    env["sb_iv"] = nc.alloc_sbuf_tensor("sb_iv", [128, HKV * NU * 4], I32).ap()
    env["sb_iq"] = nc.alloc_sbuf_tensor("sb_iq", [128, HKV * GQ * NU], I32).ap()
    env["sb_ixn"] = nc.alloc_sbuf_tensor("sb_ixn", [128, 2 * EC], I32).ap()
    env["sb_sel"] = nc.alloc_sbuf_tensor("sb_sel", [128, NU * 2], F32).ap()
    env["sb_kn2"] = nc.alloc_sbuf_tensor("sb_kn2", [1, HKV], F32).ap()
    env["sb_qn2"] = nc.alloc_sbuf_tensor("sb_qn2", [1, HQ * 2], F32).ap()

    env["IDB"] = sb_const.ap()[:, 0:128]
    env["ONESC"] = sb_const.ap()[:, 128:256]
    env["ROTM"] = sb_const.ap()[:, 256:384]

    gstk = contextlib.ExitStack()
    with gstk:
        env["gstk"] = gstk
        sC = S(gstk, nc, "sC")
        env["sC"] = sC

        with nc.Block() as blk:
            @blk.sync
            def _(sync):
                for dst, src in ((sb_const.ap()[:, 0:128], ident_b),
                                 (sb_const.ap()[:, 128:256], ones_c),
                                 (sb_const.ap()[:, 256:384], rotmat),
                                 (env["sb_identf"][:, :], ident_f),
                                 (env["sb_onesr"][:, :], ones_r),
                                 (env["sb_mask"][:, :], maskT),
                                 (env["sb_ik"][:, :], ik_t),
                                 (env["sb_iv"][:, :], iv_t),
                                 (env["sb_iq"][:, :], iq_t),
                                 (env["sb_ixn"][:, :], ixn_t),
                                 (env["sb_sel"][:, :], sel_t)):
                    sync.dma_start(out=dst, in_=src).then_inc(sC.h, 16)
                sC.n = 11 * 16

        terms = [sC]
        if "p1" in phases:
            with contextlib.ExitStack() as pstk, nc.named_scope("p1"):
                env["pstk"] = pstk
                _barrier(nc, terms)
                terms += _phase1(env)
        if "p1b" in phases:
            with contextlib.ExitStack() as pstk, nc.named_scope("p1b"):
                env["pstk"] = pstk
                _barrier(nc, terms)
                terms += _phase1b(env)
        if "p2" in phases or "p3a" in phases:
            with contextlib.ExitStack() as cstk:
                env["sb_ctxfm"] = cstk.enter_context(
                    nc.sbuf_tensor("sb_ctxfm", [128, HQ * 2 * TB], BF16))
                if "p2" in phases:
                    with contextlib.ExitStack() as pstk, nc.named_scope("p2"):
                        env["pstk"] = pstk
                        _barrier(nc, terms)
                        terms += _phase2(env)
                if "p3a" in phases:
                    with contextlib.ExitStack() as pstk, nc.named_scope("p3a"):
                        env["pstk"] = pstk
                        _barrier(nc, terms)
                        terms += _phase3a(env)
        if "p3b" in phases:
            with contextlib.ExitStack() as pstk, nc.named_scope("p3b"):
                env["pstk"] = pstk
                _barrier(nc, terms)
                terms += _phase3b(env)
        if "p3c" in phases:
            with contextlib.ExitStack() as pstk, nc.named_scope("p3c"):
                env["pstk"] = pstk
                _barrier(nc, terms)
                terms += _phase3c(env)

        _barrier(nc, terms)

    return nc



def _phase1(env):
    nc, stk, geo, sC = env["nc"], env["pstk"], env["geo"], env["sC"]
    gstk = env["gstk"]
    NB = geo["NB"]
    NG = 4 * NB
    x_t = env["x_full"].rearrange("(g p) e -> g p e", p=128)
    xnw = env["xn_scr"].rearrange("(b ec p) t -> b p ec t", ec=EC, p=128)
    kw = env["k_scr"].rearrange("(h b p) t -> b p h t", h=HKV, p=128)
    vw = env["v_scr"]
    cos_k, sin_k = env["cos_k"], env["sin_k"]
    IDB, ONESC, ROTM = env["IDB"], env["ONESC"], env["ROTM"]
    sb_kn2 = env["sb_kn2"]

    xbuf = stk.enter_context(nc.sbuf_tensor("p1_xbuf", [128, 2 * E], BF16)).ap()
    xntok = stk.enter_context(nc.sbuf_tensor("p1_xntok", [128, 2 * E], BF16)).ap()
    stats = stk.enter_context(nc.sbuf_tensor("p1_stats", [128, 16], F32)).ap()
    xnfm = stk.enter_context(nc.sbuf_tensor("p1_xnfm", [128, EC * TB], BF16)).ap()
    wkres = stk.enter_context(nc.sbuf_tensor("p1_wk", [128, EC * 1024], BF16)).ap()
    wvch = stk.enter_context(nc.sbuf_tensor("p1_wv", [128, 2 * 8 * TB], BF16)).ap()
    csb = stk.enter_context(nc.sbuf_tensor("p1_cs", [128, 2 * TB], F32)).ap()
    kraw = stk.enter_context(nc.sbuf_tensor("p1_kraw", [128, 2 * TB], BF16)).ap()
    t1f = stk.enter_context(nc.sbuf_tensor("p1_t1", [128, TB], F32)).ap()
    t2f = stk.enter_context(nc.sbuf_tensor("p1_t2", [128, TB], F32)).ap()
    kstage = stk.enter_context(nc.sbuf_tensor("p1_kst", [128, HKV * TB], BF16)).ap()
    ksqb = stk.enter_context(nc.sbuf_tensor("p1_ksqb", [128, HKV * TB], BF16)).ap()
    vstage = stk.enter_context(nc.sbuf_tensor("p1_vst", [128, 8 * TB], BF16)).ap()
    sc11 = stk.enter_context(nc.sbuf_tensor("p1_sc11", [1, 16], F32)).ap()

    ptrA = stk.enter_context(nc.psum_tensor("p1_ptrA", [128, TB], BF16))
    pk = [stk.enter_context(nc.psum_tensor(f"p1_pk{i}", [128, TB], F32))
          for i in range(4)]
    prot = [stk.enter_context(nc.psum_tensor(f"p1_prot{i}", [128, TB], F32))
            for i in range(2)]
    pn = stk.enter_context(nc.psum_tensor("p1_pn", [1, TB], F32))

    sX = [S(gstk, nc, "p1_sX0"), S(gstk, nc, "p1_sX1")]
    sWk = S(gstk, nc, "p1_sWk")
    sWv = [S(gstk, nc, "p1_sWv0"), S(gstk, nc, "p1_sWv1")]
    sWvu = S(gstk, nc, "p1_sWvu")
    sCs = S(gstk, nc, "p1_sCs")
    sSq = S(gstk, nc, "p1_sSq")
    sLn = S(gstk, nc, "p1_sLn")
    sHm = S(gstk, nc, "p1_sHm")
    sCmb = S(gstk, nc, "p1_sCmb")
    sRst = S(gstk, nc, "p1_sRst")
    sNm = S(gstk, nc, "p1_sNm")
    sXn = S(gstk, nc, "p1_sXn")
    sTrp = S(gstk, nc, "p1_sTrp")
    sTre = S(gstk, nc, "p1_sTre")
    sKp = S(gstk, nc, "p1_sKp")
    sKe = S(gstk, nc, "p1_sKe")
    sRot = S(gstk, nc, "p1_sRot")
    sRp = S(gstk, nc, "p1_sRp")
    sSqb = S(gstk, nc, "p1_sSqb")
    sKn = S(gstk, nc, "p1_sKn")
    sKnv = S(gstk, nc, "p1_sKnv")
    sKw = S(gstk, nc, "p1_sKw")
    sVe = S(gstk, nc, "p1_sVe")
    sVw = S(gstk, nc, "p1_sVw")
    sXw = S(gstk, nc, "p1_sXw")
    sVc = S(gstk, nc, "p1_sVc")
    env["p1_sXw"] = sXw

    with nc.Block() as blk:
        @blk.gpsimd
        def _(gp):
            gp.dma_start(out=wkres.rearrange("p (ec d) -> p ec d", d=1024),
                         in_=env["wkT"].rearrange("(ec p) d -> p ec d", p=128)
                         ).then_inc(sWk.h, 16)
            sWk.n = 16
            wvr = env["wvT"].rearrange("(ec8 e2 p) d -> ec8 p e2 d", p=128, e2=8)
            for b in range(NB):
                for g in range(4 * b, 4 * b + 4):
                    if g >= 2:
                        gp.wait_ge(sXn.h, g - 1)
                    gp.dma_start(out=xbuf[:, (g % 2) * E:(g % 2 + 1) * E],
                                 in_=x_t[g, :, :]).then_inc(sX[g % 2].h, 16)
                for w in range(2):
                    for e8 in range(4):
                        ch = b * 8 + w * 4 + e8
                        if ch >= 2:
                            gp.wait_ge(sWvu.h, ch - 1)
                        gp.dma_start(
                            out=wvch.rearrange("p (c e2 d) -> p c e2 d",
                                               c=2, e2=8)[:, ch % 2],
                            in_=wvr[e8, :, :, w * TB:(w + 1) * TB]
                        ).then_inc(sWv[ch % 2].h, 16)
            sX[0].n = (NG + 1) // 2 * 16
            sX[1].n = NG // 2 * 16
            sWv[0].n = NB * 4 * 16
            sWv[1].n = NB * 4 * 16

        @blk.sync
        def _(sync):
            for b in range(NB):
                if b > 0:
                    sync.wait_ge(sRp.h, 8 * b)
                sync.dma_start(out=csb[:, 0:TB],
                               in_=cos_k[:, b * TB:(b + 1) * TB]).then_inc(sCs.h, 16)
                sync.dma_start(out=csb[:, TB:2 * TB],
                               in_=sin_k[:, b * TB:(b + 1) * TB]).then_inc(sCs.h, 16)
                sync.wait_ge(sRp.h, 8 * (b + 1))
                sync.dma_start(out=kw[b],
                               in_=kstage.rearrange("p (h t) -> p h t", t=TB)
                               ).then_inc(sKw.h, 16)
                for w in range(2):
                    for s in range(4):
                        sync.wait_ge(sVe.h, 8 * b + w * 4 + s + 1)
                        sync.dma_start(
                            out=vw[(b * 4 + s) * 128:(b * 4 + s + 1) * 128,
                                   w * TB:(w + 1) * TB],
                            in_=vstage[:, (w * 4 + s) * TB:(w * 4 + s + 1) * TB]
                        ).then_inc(sVw.h, 16)
                sync.wait_ge(sTre.h, 32 * (b + 1))
                for q in range(4):
                    sync.dma_start(
                        out=xnw[b][:, q * 8:(q + 1) * 8, :],
                        in_=xnfm.rearrange("p (ec t) -> p ec t",
                                           t=TB)[:, q * 8:(q + 1) * 8, :]
                    ).then_inc(sXw.h, 16)
            sCs.n = NB * 32
            sKw.n = NB * 16
            sVw.n = NB * 8 * 16
            sXw.n = NB * 4 * 16

        @blk.scalar
        def _(sc):
            for g in range(NG):
                par = g % 2
                xb = xbuf[:, par * E:(par + 1) * E]
                xo = xntok[:, par * E:(par + 1) * E]
                st = stats[:, par * 8:(par + 1) * 8]
                sc.wait_ge(sX[g % 2].h, (g // 2 + 1) * 16)
                if g >= 2:
                    sc.wait_ge(sCmb.h, g - 1)
                    sc.wait_ge(sTrp.h, 8 * (g - 1))
                sc.activation(xo, xb, AF.Square,
                              accum_out=st[:, 1:2]).then_inc(sSq.h, 1)
                sc.wait_ge(sCmb.h, g + 1)
                if g >= 2:
                    sc.wait_ge(sHm.h, g - 1)
                sc.activation(st[:, 4:5], st[:, 3:4], AF.Ln).then_inc(sLn.h, 1)
                sc.wait_ge(sHm.h, g + 1)
                if g >= 2:
                    sc.wait_ge(sXn.h, g - 1)
                sc.activation(st[:, 5:6], st[:, 4:5],
                              AF.Exp).then_inc(sRst.h, 1)
            sSq.n = NG
            sLn.n = NG
            sRst.n = NG

        @blk.vector
        def _(v):
            def vsync(ins):
                sVc.n += 1
                ins.then_inc(sVc.h, 1)
                v.wait_ge(sVc.h, sVc.n)
            for b in range(NB):
                for i in range(4):
                    g = 4 * b + i
                    par = g % 2
                    xb = xbuf[:, par * E:(par + 1) * E]
                    xo = xntok[:, par * E:(par + 1) * E]
                    st = stats[:, par * 8:(par + 1) * 8]
                    v.wait_ge(sX[g % 2].h, (g // 2 + 1) * 16)
                    vsync(v.reduce_sum(st[:, 0:1], xb, axis=AX.X))
                    v.wait_ge(sSq.h, g + 1)
                    vsync(v.tensor_scalar_mul(st[:, 2:3], st[:, 0:1], 1.0 / E))
                    vsync(v.tensor_mul(st[:, 7:8], st[:, 2:3], st[:, 2:3]))
                    vsync(v.scalar_tensor_tensor(st[:, 3:4], st[:, 1:2],
                                                 1.0 / E, st[:, 7:8],
                                                 ALU.mult, ALU.subtract))
                    v.tensor_scalar_add(st[:, 3:4], st[:, 3:4],
                                        float(EPS)).then_inc(sCmb.h, 1)
                    v.wait_ge(sLn.h, g + 1)
                    v.tensor_scalar_mul(st[:, 4:5], st[:, 4:5],
                                        -0.5).then_inc(sHm.h, 1)
                    v.wait_ge(sRst.h, g + 1)
                    vsync(v.scalar_tensor_tensor(st[:, 6:7], st[:, 2:3], -1.0,
                                                 st[:, 5:6], ALU.mult,
                                                 ALU.mult))
                    v.tensor_scalar(xo, xb, st[:, 5:6], st[:, 6:7],
                                    ALU.mult, ALU.add).then_inc(sXn.h, 1)
                    for grp in range(8):
                        G = 8 * g + grp
                        v.wait_ge(sTrp.h, G + 1)
                        if b > 0 and i == 0 and grp == 0:
                            v.wait_ge(sWvu.h, 8 * b)
                            v.wait_ge(sXw.h, 64 * b)
                        dst = xnfm.rearrange("p (ec t) -> p ec t", t=TB)[
                            :, 4 * grp:4 * grp + 4, i * 128:(i + 1) * 128]
                        v.tensor_copy(dst, ptrA.ap().rearrange(
                            "p (j t) -> p j t", t=128)).then_inc(sTre.h, 1)
                for h in range(HKV):
                    grp, hh = divmod(h, 4)
                    idx = 8 * b + h
                    par = h % 2
                    v.wait_ge(sKp.h, 2 * b + grp + 1)
                    if idx >= 2:
                        v.wait_ge(sRot.h, idx - 1)
                    v.tensor_copy(kraw[:, par * TB:(par + 1) * TB],
                                  pk[hh].ap()).then_inc(sKe.h, 1)
                    v.wait_ge(sKe.h, idx + 1)
                    v.wait_ge(sCs.h, 32 * b + 32)
                    vsync(v.tensor_mul(t1f[:, :], kraw[:, par * TB:(par + 1) * TB],
                                       csb[:, 0:TB]))
                    v.wait_ge(sRot.h, idx + 1)
                    vsync(v.tensor_mul(t2f[:, :], prot[par].ap(),
                                       csb[:, TB:2 * TB]))
                    if b > 0:
                        v.wait_ge(sKw.h, 16 * b)
                    v.tensor_add(kstage[:, h * TB:(h + 1) * TB], t1f[:, :],
                                 t2f[:, :]).then_inc(sRp.h, 1)
                v.wait_ge(sRp.h, 8 * (b + 1))
                if b > 0:
                    v.wait_ge(sKn.h, 8 * b)
                v.tensor_mul(ksqb[:, :], kstage[:, :],
                             kstage[:, :]).then_inc(sSqb.h, 1)
                for w in range(2):
                    for s in range(4):
                        hn = w * 4 + s
                        v.wait_ge(sKn.h, 8 * b + hn + 1)
                        vsync(v.reduce_max(sc11[:, 0:1], pn.ap(), axis=AX.X))
                        if b == 0:
                            v.tensor_copy(sb_kn2[:, hn:hn + 1],
                                          sc11[:, 0:1]).then_inc(sKnv.h, 1)
                        else:
                            v.tensor_max(sb_kn2[:, hn:hn + 1],
                                         sb_kn2[:, hn:hn + 1],
                                         sc11[:, 0:1]).then_inc(sKnv.h, 1)
                    for s in range(4):
                        v.wait_ge(sWvu.h, 8 * b + 4 * (w + 1))
                        if b > 0:
                            v.wait_ge(sVw.h, 128 * b)
                        v.tensor_copy(vstage[:, (w * 4 + s) * TB:
                                             (w * 4 + s + 1) * TB],
                                      pk[s].ap()).then_inc(sVe.h, 1)
            sCmb.n = NG
            sHm.n = NG
            sXn.n = NG
            sTre.n = NG * 8
            sKe.n = NB * 8
            sRp.n = NB * 8
            sSqb.n = NB
            sKnv.n = NB * 8
            sVe.n = NB * 8

        @blk.tensor
        def _(t):
            t.wait_ge(sWk.h, 16)
            for b in range(NB):
                for i in range(4):
                    g = 4 * b + i
                    par = g % 2
                    t.wait_ge(sXn.h, g + 1)
                    for grp in range(8):
                        G = 8 * g + grp
                        if G >= 1:
                            t.wait_ge(sTre.h, G)
                        for j in range(4):
                            ins = t.transpose(
                                ptrA[:, j * 128:(j + 1) * 128],
                                xntok[:, par * E + (grp * 4 + j) * 128:
                                      par * E + (grp * 4 + j + 1) * 128],
                                IDB)
                        ins.then_inc(sTrp.h, 1)
                t.wait_ge(sVe.h, 8 * b)
                for grp in range(2):
                    if grp == 1:
                        t.wait_ge(sKe.h, 8 * b + 4)
                    for ec in range(EC):
                        for hh in range(4):
                            h = grp * 4 + hh
                            ins = t.matmul(pk[hh].ap(),
                                           wkres[:, ec * 1024 + h * 128:
                                                 ec * 1024 + (h + 1) * 128],
                                           xnfm[:, ec * TB:(ec + 1) * TB],
                                           start=(ec == 0), stop=(ec == EC - 1))
                    ins.then_inc(sKp.h, 1)
                    for h in range(grp * 4, grp * 4 + 4):
                        idx = 8 * b + h
                        par = h % 2
                        t.wait_ge(sKe.h, idx + 1)
                        if idx >= 2:
                            t.wait_ge(sRp.h, idx - 1)
                        t.matmul(prot[par].ap(), ROTM,
                                 kraw[:, par * TB:(par + 1) * TB], start=True,
                                 stop=True).then_inc(sRot.h, 1)
                t.wait_ge(sKe.h, 8 * (b + 1))
                for w in range(2):
                    if w == 1:
                        t.wait_ge(sVe.h, 8 * b + 4)
                    for e8 in range(4):
                        hn = w * 4 + e8
                        t.wait_ge(sSqb.h, b + 1)
                        if 8 * b + hn >= 1:
                            t.wait_ge(sKnv.h, 8 * b + hn)
                        t.matmul(pn.ap(), ONESC[:, 0:1],
                                 ksqb[:, hn * TB:(hn + 1) * TB], start=True,
                                 stop=True).then_inc(sKn.h, 1)
                        ch = b * 8 + w * 4 + e8
                        t.wait_ge(sWv[ch % 2].h, 16 * (ch // 2 + 1))
                        for e2 in range(8):
                            ec = 8 * e8 + e2
                            for s in range(4):
                                ins = t.matmul(
                                    pk[s].ap(),
                                    xnfm[:, ec * TB + s * 128:
                                         ec * TB + (s + 1) * 128],
                                    wvch[:, (ch % 2) * 8 * TB + e2 * TB:
                                         (ch % 2) * 8 * TB + (e2 + 1) * TB],
                                    start=(ec == 0), stop=(ec == EC - 1))
                        ins.then_inc(sWvu.h, 1)
            sTrp.n = NG * 8
            sKp.n = 2 * NB
            sRot.n = 8 * NB
            sKn.n = 8 * NB
            sWvu.n = 8 * NB

    return [sKw, sVw, sXw, sKnv, sKn, sSqb, sX[0], sX[1], sWv[0], sWv[1],
            sCs, sKe, sRp, sVe, sTre, sXn, sSq, sLn, sHm, sRst, sCmb, sTrp,
            sKp, sRot, sWvu, sWk, sVc]



def _phase1b(env):
    nc, stk, geo = env["nc"], env["pstk"], env["geo"]
    gstk = env["gstk"]
    IDB, ONESC, ROTM = env["IDB"], env["ONESC"], env["ROTM"]
    sb_qn2, sb_ixn = env["sb_qn2"], env["sb_ixn"]
    qw = env["q_scr"]

    xno = stk.enter_context(nc.sbuf_tensor("pb_xno", [128, 2 * EC * TB], BF16)).ap()
    wqc = stk.enter_context(nc.sbuf_tensor("pb_wqc", [128, 2 * EC * TB], BF16)).ap()
    csq = stk.enter_context(nc.sbuf_tensor("pb_csq", [128, 4 * TB], F32)).ap()
    qraw = stk.enter_context(nc.sbuf_tensor("pb_qraw", [128, 2 * TB], BF16)).ap()
    t1f = stk.enter_context(nc.sbuf_tensor("pb_t1", [128, TB], F32)).ap()
    t2f = stk.enter_context(nc.sbuf_tensor("pb_t2", [128, TB], F32)).ap()
    qstage = stk.enter_context(nc.sbuf_tensor("pb_qst", [128, 2 * TB], BF16)).ap()
    qsq = stk.enter_context(nc.sbuf_tensor("pb_qsq", [128, TB], BF16)).ap()
    sc11 = stk.enter_context(nc.sbuf_tensor("pb_sc11", [1, 16], F32)).ap()

    pqs = [stk.enter_context(nc.psum_tensor(f"pb_pq{i}", [128, TB], F32))
           for i in range(2)]
    prot = [stk.enter_context(nc.psum_tensor(f"pb_prot{i}", [128, TB], F32))
            for i in range(2)]
    pn = stk.enter_context(nc.psum_tensor("pb_pn", [1, TB], F32))

    sG = S(gstk, nc, "pb_sG")
    sW = [S(gstk, nc, "pb_sW0"), S(gstk, nc, "pb_sW1")]
    sCsq = S(gstk, nc, "pb_sCsq")
    sQp = S(gstk, nc, "pb_sQp")
    sQe = S(gstk, nc, "pb_sQe")
    sRot = S(gstk, nc, "pb_sRot")
    sRp = S(gstk, nc, "pb_sRp")
    sVc = S(gstk, nc, "pb_sVc")
    sQsq = S(gstk, nc, "pb_sQsq")
    sQn = S(gstk, nc, "pb_sQn")
    sQnv = S(gstk, nc, "pb_sQnv")
    sQw = [S(gstk, nc, "pb_sQw0"), S(gstk, nc, "pb_sQw1")]
    env["pb_sQw"] = sQw[0]

    with nc.Block() as blk:
        @blk.gpsimd
        def _(gp):
            for col in range(2 * EC):
                gp.indirect_dma_start(
                    xno[:, col * TB:(col + 1) * TB], None,
                    env["xn_scr"],
                    bass.IndirectOffsetOnAxis(sb_ixn[:, col:col + 1], 0)
                ).then_inc(sG.h, 16)
            sG.n = 2 * EC * 16
            for qc in range(8):
                if qc >= 2:
                    gp.wait_ge(sQp.h, (qc - 1) * 8)
                gp.dma_start(
                    out=wqc.rearrange("p (c ec d) -> p c ec d", c=2,
                                      d=TB)[:, qc % 2],
                    in_=env["wqTr"][qc]).then_inc(sW[qc % 2].h, 16)
            sW[0].n = 4 * 16
            sW[1].n = 4 * 16

        @blk.sync
        def _(sync):
            sync.dma_start(out=csq[:, 0:2 * TB],
                           in_=env["cos_q"]).then_inc(sCsq.h, 16)
            sync.dma_start(out=csq[:, 2 * TB:4 * TB],
                           in_=env["sin_q"]).then_inc(sCsq.h, 16)
            sCsq.n = 32
            for j in range(64):
                qh, a = divmod(j, 2)
                sync.wait_ge(sRp.h, j + 1)
                sync.dma_start(out=qw[(qh * 2 + a) * 128:(qh * 2 + a + 1) * 128, :],
                               in_=qstage[:, (j % 2) * TB:(j % 2 + 1) * TB]
                               ).then_inc(sQw[j % 2].h, 16)
            sQw[0].n = 32 * 16
            sQw[1].n = 32 * 16

        @blk.tensor
        def _(t):
            t.wait_ge(sG.h, 2 * EC * 16)
            def qn_mm(jj):
                t.wait_ge(sQsq.h, jj + 1)
                if jj >= 1:
                    t.wait_ge(sQnv.h, jj)
                t.matmul(pn.ap(), ONESC[:, 0:1], qsq[:, :], start=True,
                         stop=True).then_inc(sQn.h, 1)
            for qc in range(8):
                t.wait_ge(sW[qc % 2].h, 16 * (qc // 2 + 1))
                for hh in range(4):
                    for a in range(2):
                        j = qc * 8 + hh * 2 + a
                        if j >= 1:
                            qn_mm(j - 1)
                        pb = pqs[j % 2]
                        if j >= 2:
                            t.wait_ge(sQe.h, j - 1)
                        for ec in range(EC):
                            ins = t.matmul(
                                pb.ap(),
                                wqc[:, (qc % 2) * EC * TB + ec * TB + hh * 128:
                                    (qc % 2) * EC * TB + ec * TB + (hh + 1) * 128],
                                xno[:, (a * EC + ec) * TB:(a * EC + ec + 1) * TB],
                                start=(ec == 0), stop=(ec == EC - 1))
                        ins.then_inc(sQp.h, 1)
                        t.wait_ge(sQe.h, j + 1)
                        if j >= 2:
                            t.wait_ge(sRp.h, j - 1)
                        t.matmul(prot[j % 2].ap(), ROTM,
                                 qraw[:, (j % 2) * TB:(j % 2 + 1) * TB],
                                 start=True, stop=True).then_inc(sRot.h, 1)
            qn_mm(63)
            sQp.n = 64
            sRot.n = 64
            sQn.n = 64

        @blk.vector
        def _(v):
            def vsync(ins):
                sVc.n += 1
                ins.then_inc(sVc.h, 1)
                v.wait_ge(sVc.h, sVc.n)
            v.wait_ge(sCsq.h, 32)
            for j in range(64):
                a = j % 2
                pb = pqs[j % 2]
                v.wait_ge(sQp.h, j + 1)
                if j >= 2:
                    v.wait_ge(sRot.h, j - 1)
                v.tensor_copy(qraw[:, (j % 2) * TB:(j % 2 + 1) * TB],
                              pb.ap()).then_inc(sQe.h, 1)
                v.wait_ge(sQe.h, j + 1)
                vsync(v.tensor_mul(t1f[:, :],
                                   qraw[:, (j % 2) * TB:(j % 2 + 1) * TB],
                                   csq[:, a * TB:(a + 1) * TB]))
                v.wait_ge(sRot.h, j + 1)
                vsync(v.tensor_mul(t2f[:, :], prot[j % 2].ap(),
                                   csq[:, 2 * TB + a * TB:2 * TB + (a + 1) * TB]))
                if j >= 2:
                    v.wait_ge(sQw[j % 2].h, 16 * (j // 2))
                v.tensor_add(qstage[:, (j % 2) * TB:(j % 2 + 1) * TB],
                             t1f[:, :], t2f[:, :]).then_inc(sRp.h, 1)
                if j >= 1:
                    v.wait_ge(sQn.h, j)
                    vsync(v.reduce_max(sc11[:, 0:1], pn.ap(), axis=AX.X))
                    v.tensor_copy(sb_qn2[:, j - 1:j],
                                  sc11[:, 0:1]).then_inc(sQnv.h, 1)
                v.wait_ge(sRp.h, j + 1)
                if j >= 1:
                    v.wait_ge(sQn.h, j)
                v.tensor_mul(qsq[:, :],
                             qstage[:, (j % 2) * TB:(j % 2 + 1) * TB],
                             qstage[:, (j % 2) * TB:(j % 2 + 1) * TB]
                             ).then_inc(sQsq.h, 1)
            v.wait_ge(sQn.h, 64)
            vsync(v.reduce_max(sc11[:, 0:1], pn.ap(), axis=AX.X))
            v.tensor_copy(sb_qn2[:, 63:64], sc11[:, 0:1]).then_inc(sQnv.h, 1)
            sQe.n = 64
            sRp.n = 64
            sQsq.n = 64
            sQnv.n = 64

    return [sQw[0], sQw[1], sQnv, sQn, sQsq, sG, sW[0], sW[1], sCsq, sQp,
            sQe, sRot, sRp, sVc]



def _phase2(env):
    nc, stk, geo = env["nc"], env["pstk"], env["geo"]
    gstk = env["gstk"]
    NU = geo["NU"]
    IDB, ONESC = env["IDB"], env["ONESC"]
    sb_kn2, sb_qn2, sb_sel, sb_mask = (env["sb_kn2"], env["sb_qn2"],
                                       env["sb_sel"], env["sb_mask"])
    sb_ctxfm = env["sb_ctxfm"].ap()
    sb_identf, sb_onesr = env["sb_identf"], env["sb_onesr"]

    kslot = stk.enter_context(nc.sbuf_tensor("p2_ks", [128, 2 * TB], BF16)).ap()
    vslot = stk.enter_context(nc.sbuf_tensor("p2_vs", [128, 2 * TB], BF16)).ap()
    qslot = stk.enter_context(nc.sbuf_tensor("p2_qs", [128, 2 * GQ * TB], BF16)).ap()
    pbuf = stk.enter_context(nc.sbuf_tensor("p2_pT", [128, 4 * TB], BF16)).ap()
    cacc = stk.enter_context(nc.sbuf_tensor("p2_cacc", [128, 8 * TB], F32)).ap()
    lacc = stk.enter_context(nc.sbuf_tensor("p2_lacc", [128, 8 * TB], F32)).ap()
    cbias = stk.enter_context(nc.sbuf_tensor("p2_cb", [128, 8], F32)).ap()
    sc11 = stk.enter_context(nc.sbuf_tensor("p2_sc11", [1, 16], F32)).ap()
    scb = stk.enter_context(nc.sbuf_tensor("p2_scb", [1, 16], BF16)).ap()
    linv = stk.enter_context(nc.sbuf_tensor("p2_linv", [128, 2 * TB],
                                             BF16)).ap()

    psT = [stk.enter_context(nc.psum_tensor(f"p2_psT{i}", [128, TB], F32))
           for i in range(2)]
    pctx = [stk.enter_context(nc.psum_tensor(f"p2_pctx{i}", [128, TB], F32))
            for i in range(2)]
    pl = [stk.enter_context(nc.psum_tensor(f"p2_pl{i}", [128, TB], F32))
          for i in range(2)]
    pbc = stk.enter_context(nc.psum_tensor("p2_pbc", [128, 1], F32))

    sG = [S(gstk, nc, "p2_sG0"), S(gstk, nc, "p2_sG1")]
    sBi = S(gstk, nc, "p2_sBi")
    sBb = S(gstk, nc, "p2_sBb")
    sBv = S(gstk, nc, "p2_sBv")
    sSt = S(gstk, nc, "p2_sSt")
    sEx = S(gstk, nc, "p2_sEx")
    sMk = S(gstk, nc, "p2_sMk")
    sLm = S(gstk, nc, "p2_sLm")
    sAc = S(gstk, nc, "p2_sAc")
    sMs = S(gstk, nc, "p2_sMs")
    sFt = S(gstk, nc, "p2_sFt")
    sFv = S(gstk, nc, "p2_sFv")
    sFin = S(gstk, nc, "p2_sFin")
    env["p2_sFin"] = sFin

    sched = geo["sched"]

    def bias_idx(h, qq, a):
        return h * 8 + qq * 2 + a

    with nc.Block() as blk:
        @blk.gpsimd
        def _(gp):
            for h in range(HKV):
                for u in range(NU):
                    uu = h * NU + u
                    if uu >= 2:
                        gp.wait_ge(sLm.h, 4 * GQ * (uu - 1))
                    par = uu % 2
                    gp.indirect_dma_start(
                        kslot[:, par * TB:(par + 1) * TB], None, env["k_scr"],
                        bass.IndirectOffsetOnAxis(
                            env["sb_ik"][:, uu:uu + 1], 0)).then_inc(sG[uu % 2].h, 16)
                    for s in range(4):
                        gp.indirect_dma_start(
                            vslot[:, par * TB + s * 128:par * TB + (s + 1) * 128],
                            None,
                            env["v_scr"].rearrange("r (h d) -> (r h) d", d=128),
                            bass.IndirectOffsetOnAxis(
                                env["sb_iv"][:, uu * 4 + s:uu * 4 + s + 1], 0)
                        ).then_inc(sG[uu % 2].h, 16)
                    for qq in range(GQ):
                        gp.indirect_dma_start(
                            qslot[:, (par * GQ + qq) * TB:(par * GQ + qq + 1) * TB],
                            None, env["q_scr"],
                            bass.IndirectOffsetOnAxis(
                                env["sb_iq"][:, uu * GQ + qq:uu * GQ + qq + 1], 0)
                        ).then_inc(sG[uu % 2].h, 16)
            NUU = HKV * NU
            sG[0].n = (NUU + 1) // 2 * 9 * 16
            sG[1].n = NUU // 2 * 9 * 16

        @blk.scalar
        def _(sc):
            nBi = nEx = 0
            for h in range(HKV):
                for qq in range(GQ):
                    for a in range(2):
                        j = bias_idx(h, qq, a)
                        sc.wait_ge(sBv.h, 4 * j + 1)
                        sc.activation(sc11[:, 3:4], sc11[:, 2:3],
                                      AF.Ln).then_inc(sBi.h, 1)
                        sc.wait_ge(sBv.h, 4 * j + 2)
                        sc.activation(sc11[:, 4:5], sc11[:, 3:4],
                                      AF.Exp).then_inc(sBi.h, 1)
                        nBi += 2
                for u in range(NU):
                    uu = h * NU + u
                    a_u = sched[u][0]
                    for qq in range(GQ):
                        j = bias_idx(h, qq, a_u)
                        kq = uu * GQ + qq
                        for s in range(4):
                            k = kq * 4 + s
                            sc.wait_ge(sSt.h, k + 1)
                            sc.wait_ge(sBv.h, 4 * (j + 1))
                            if kq >= 1:
                                sc.wait_ge(sLm.h, 4 * kq)
                            sc.activation(pbuf[:, s * TB:(s + 1) * TB],
                                          psT[k % 2].ap(), AF.Exp,
                                          bias=cbias[:, j % 8:j % 8 + 1]
                                          ).then_inc(sEx.h, 1)
                            nEx += 1
            sBi.n = nBi
            sEx.n = nEx

        @blk.tensor
        def _(t):
            nFt = 0
            for h in range(HKV):
                for qq in range(GQ):
                    for a in range(2):
                        j = bias_idx(h, qq, a)
                        t.wait_ge(sBv.h, 4 * j + 3)
                        if j >= 1:
                            t.wait_ge(sBv.h, 4 * j)
                        t.matmul(pbc.ap(), sb_onesr[:, :], scb[:, 0:1],
                                 start=True, stop=True).then_inc(sBb.h, 1)
                for u in range(NU):
                    uu = h * NU + u
                    par = uu % 2
                    t.wait_ge(sG[uu % 2].h, 16 * 9 * (uu // 2 + 1))
                    t.wait_ge(sMs.h, 16 * (h + 1))
                    if u == 0 and h > 0:
                        t.wait_ge(sFin.h, 8 * h)
                    for qq in range(GQ):
                        kq = uu * GQ + qq
                        for s in range(4):
                            k = kq * 4 + s
                            if k >= 2:
                                t.wait_ge(sEx.h, k - 1)
                            t.matmul(psT[k % 2].ap(),
                                     kslot[:, par * TB + s * 128:
                                           par * TB + (s + 1) * 128],
                                     qslot[:, (par * GQ + qq) * TB:
                                           (par * GQ + qq + 1) * TB],
                                     start=True, stop=True).then_inc(sSt.h, 1)
                        for s in range(4):
                            k = kq * 4 + s
                            if u < 2:
                                t.wait_ge(sMk.h, (h * 2 * GQ + u * GQ + qq) * 4
                                          + s + 1)
                            else:
                                t.wait_ge(sEx.h, k + 1)
                            if s == 0 and kq >= 2:
                                t.wait_ge(sAc.h, 4 * (kq - 1))
                            t.matmul(pctx[kq % 2].ap(),
                                     vslot[:, par * TB + s * 128:
                                           par * TB + (s + 1) * 128],
                                     pbuf[:, s * TB:(s + 1) * TB],
                                     start=(s == 0), stop=(s == 3))
                            t.matmul(pl[kq % 2].ap(), ONESC[:, :],
                                     pbuf[:, s * TB:(s + 1) * TB],
                                     start=(s == 0),
                                     stop=(s == 3)).then_inc(sLm.h, 1)
                t.wait_ge(sAc.h, 4 * GQ * NU * (h + 1))
                for acix in range(8):
                    gi = 8 * h + acix
                    t.wait_ge(sFv.h, gi + 1)
                    if gi >= 2:
                        t.wait_ge(sFin.h, gi - 1)
                    with nc.allow_low_precision(
                            reason="1/l broadcast; bf16 like old ctx copy"):
                        t.matmul(psT[acix % 2].ap(), sb_onesr[:, :],
                                 linv[0:1, (acix % 2) * TB:(acix % 2 + 1) * TB],
                                 start=True,
                                 stop=True).then_inc(sFt.h, 1)
                    nFt += 1
            sBb.n = HKV * GQ * 2
            sSt.n = HKV * NU * GQ * 4
            sLm.n = HKV * NU * GQ * 4
            sFt.n = nFt

        @blk.vector
        def _(v):
            nBv = nMk = nMs = nFv = 0
            for h in range(HKV):
                for z in range(8):
                    v.memset(cacc[:, z * TB:(z + 1) * TB], 0.0)
                    v.memset(lacc[:, z * TB:(z + 1) * TB],
                             0.0).then_inc(sMs.h, 2)
                    nMs += 2
                for qq in range(GQ):
                    for a in range(2):
                        j = bias_idx(h, qq, a)
                        qh = h * GQ + qq
                        if j >= 1:
                            v.wait_ge(sBi.h, 2 * j - 1)
                        v.tensor_mul(sc11[:, 2:3],
                                     sb_qn2[:, qh * 2 + a:qh * 2 + a + 1],
                                     sb_kn2[:, h:h + 1]).then_inc(sBv.h, 1)
                        v.wait_ge(sBi.h, 2 * j + 1)
                        v.tensor_scalar_mul(sc11[:, 3:4], sc11[:, 3:4],
                                            0.5).then_inc(sBv.h, 1)
                        v.wait_ge(sBi.h, 2 * j + 2)
                        if j >= 1:
                            v.wait_ge(sBb.h, j)
                        v.tensor_scalar_mul(scb[:, 0:1], sc11[:, 4:5],
                                            -1.0).then_inc(sBv.h, 1)
                        v.wait_ge(sBb.h, j + 1)
                        v.tensor_copy(cbias[:, j % 8:j % 8 + 1],
                                      pbc.ap()).then_inc(sBv.h, 1)
                        nBv += 4
                for u in range(NU):
                    uu = h * NU + u
                    for qq in range(GQ):
                        kq = uu * GQ + qq
                        if u < 2:
                            for s in range(4):
                                k = kq * 4 + s
                                v.wait_ge(sEx.h, k + 1)
                                v.tensor_mul(pbuf[:, s * TB:(s + 1) * TB],
                                             pbuf[:, s * TB:(s + 1) * TB],
                                             sb_mask[:, s * TB:(s + 1) * TB]
                                             ).then_inc(sMk.h, 1)
                                nMk += 1
                        v.wait_ge(sLm.h, 4 * (kq + 1))
                        nAc0 = 4 * kq
                        for a in range(2):
                            acix = qq * 2 + a
                            selap = sb_sel[:, u * 2 + a:u * 2 + a + 1]
                            v.scalar_tensor_tensor(
                                cacc[:, acix * TB:(acix + 1) * TB],
                                pctx[kq % 2].ap(), selap,
                                cacc[:, acix * TB:(acix + 1) * TB],
                                ALU.mult, ALU.add).then_inc(sAc.h, 1)
                            v.scalar_tensor_tensor(
                                lacc[:, acix * TB:(acix + 1) * TB],
                                pl[kq % 2].ap(), selap,
                                lacc[:, acix * TB:(acix + 1) * TB],
                                ALU.mult, ALU.add).then_inc(sAc.h, 1)
                        v.wait_ge(sAc.h, nAc0 + 4)
                for acix in range(8):
                    gi = 8 * h + acix
                    qq, a = divmod(acix, 2)
                    qh = h * GQ + qq
                    if gi >= 2:
                        v.wait_ge(sFt.h, gi - 1)
                    with nc.allow_low_precision(
                            reason="1/l in bf16; matches old bf16 ctx copy"):
                        v.reciprocal(linv[0:1,
                                          (acix % 2) * TB:(acix % 2 + 1) * TB],
                                     lacc[0:1, acix * TB:(acix + 1) * TB]
                                     ).then_inc(sFv.h, 1)
                    nFv += 1
                    v.wait_ge(sFt.h, gi + 1)
                    v.tensor_mul(sb_ctxfm[:, qh * 2 * TB + a * TB:
                                          qh * 2 * TB + (a + 1) * TB],
                                 cacc[:, acix * TB:(acix + 1) * TB],
                                 psT[acix % 2].ap()).then_inc(sFin.h, 1)
            sBv.n = nBv
            sMk.n = nMk
            sAc.n = HKV * NU * GQ * 4
            sMs.n = nMs
            sFv.n = nFv
            sFin.n = HKV * 8

        if env["dbg"] and env["ctx_scr"] is not None:
            @blk.sync
            def _(sync):
                sync.wait_ge(sFin.h, HKV * 8)
                base = sBi.n
                sync.dma_start(out=env["ctx_scr"],
                               in_=sb_ctxfm[:, :]).then_inc(sBi.h, 16)
                sBi.n = base + 16
                sync.wait_ge(sBi.h, sBi.n)

    return [sFin, sG[0], sG[1], sBi, sBb, sBv, sSt, sEx, sMk, sLm, sAc, sMs,
            sFt, sFv]



def _phase3a(env):
    nc, stk, geo = env["nc"], env["pstk"], env["geo"]
    gstk = env["gstk"]
    sb_ctxfm = env["sb_ctxfm"].ap()

    woc = stk.enter_context(nc.sbuf_tensor("p3a_wo", [128, 2 * EC * TB], BF16)).ap()
    xot = stk.enter_context(nc.sbuf_tensor("p3a_xo", [128, 2 * TB], F32)).ap()
    ot = stk.enter_context(nc.sbuf_tensor("p3a_ot", [128, 2 * TB], F32)).ap()
    po = [stk.enter_context(nc.psum_tensor(f"p3a_po{i}", [128, TB], F32))
          for i in range(2)]

    sW = [S(gstk, nc, "p3a_sW0"), S(gstk, nc, "p3a_sW1")]
    sXo = [S(gstk, nc, "p3a_sXo0"), S(gstk, nc, "p3a_sXo1")]
    sPo = S(gstk, nc, "p3a_sPo")
    sEv = S(gstk, nc, "p3a_sEv")
    sOw = [S(gstk, nc, "p3a_sOw0"), S(gstk, nc, "p3a_sOw1")]
    env["p3a_sOw"] = sOw[0]

    with nc.Block() as blk:
        @blk.gpsimd
        def _(gp):
            for ew in range(8):
                if ew >= 2:
                    gp.wait_ge(sPo.h, 8 * (ew - 1))
                gp.dma_start(
                    out=woc.rearrange("p (c ec d) -> p c ec d", c=2,
                                      d=TB)[:, ew % 2],
                    in_=env["woTr"][ew]).then_inc(sW[ew % 2].h, 16)
            sW[0].n = 4 * 16
            sW[1].n = 4 * 16

        @blk.sync
        def _(sync):
            for ew in range(8):
                for tt in range(8):
                    j = ew * 8 + tt
                    if j >= 2:
                        sync.wait_ge(sEv.h, j - 1)
                    sync.dma_start(
                        out=xot[:, (j % 2) * TB:(j % 2 + 1) * TB],
                        in_=env["x_own"][tt * 128:(tt + 1) * 128,
                                         ew * TB:(ew + 1) * TB]
                    ).then_inc(sXo[j % 2].h, 16)
            sXo[0].n = 32 * 16
            sXo[1].n = 32 * 16

        @blk.tensor
        def _(t):
            for ew in range(8):
                t.wait_ge(sW[ew % 2].h, 16 * (ew // 2 + 1))
                for tt in range(8):
                    j = ew * 8 + tt
                    pb = po[j % 2]
                    if j >= 2:
                        t.wait_ge(sEv.h, j - 1)
                    for hdc in range(EC):
                        ins = t.matmul(
                            pb.ap(),
                            sb_ctxfm[:, hdc * 2 * TB + tt * 128:
                                     hdc * 2 * TB + (tt + 1) * 128],
                            woc[:, (ew % 2) * EC * TB + hdc * TB:
                                (ew % 2) * EC * TB + (hdc + 1) * TB],
                            start=(hdc == 0), stop=(hdc == EC - 1))
                    ins.then_inc(sPo.h, 1)
            sPo.n = 64

        @blk.vector
        def _(v):
            for ew in range(8):
                for tt in range(8):
                    j = ew * 8 + tt
                    v.wait_ge(sPo.h, j + 1)
                    v.wait_ge(sXo[j % 2].h, 16 * (j // 2 + 1))
                    if j >= 2:
                        v.wait_ge(sOw[j % 2].h, 16 * (j // 2))
                    v.tensor_add(ot[:, (j % 2) * TB:(j % 2 + 1) * TB],
                                 po[j % 2].ap(),
                                 xot[:, (j % 2) * TB:(j % 2 + 1) * TB]
                                 ).then_inc(sEv.h, 1)
            sEv.n = 64

        @blk.scalar
        def _(sc):
            for ew in range(8):
                for tt in range(8):
                    j = ew * 8 + tt
                    sc.wait_ge(sEv.h, j + 1)
                    sc.dma_start(out=env["out"][tt * 128:(tt + 1) * 128,
                                                ew * TB:(ew + 1) * TB],
                                 in_=ot[:, (j % 2) * TB:(j % 2 + 1) * TB]
                                 ).then_inc(sOw[j % 2].h, 16)
            sOw[0].n = 32 * 16
            sOw[1].n = 32 * 16

    return [sOw[0], sOw[1], sW[0], sW[1], sXo[0], sXo[1], sPo, sEv]



def _phase3b(env):
    nc, stk, geo = env["nc"], env["pstk"], env["geo"]
    gstk = env["gstk"]
    FC = geo["FC"]
    sb_ixn = env["sb_ixn"]

    xno = stk.enter_context(nc.sbuf_tensor("p3b_xno", [128, 2 * EC * TB], BF16)).ap()
    gch = stk.enter_context(nc.sbuf_tensor("p3b_g", [128, 2 * EC * 128], BF16)).ap()
    uch = stk.enter_context(nc.sbuf_tensor("p3b_u", [128, 2 * EC * 128], BF16)).ap()
    gsb = stk.enter_context(nc.sbuf_tensor("p3b_gs", [128, FC], F32)).ap()
    usb = stk.enter_context(nc.sbuf_tensor("p3b_us", [128, FC], F32)).ap()
    sg = stk.enter_context(nc.sbuf_tensor("p3b_sg", [128, 2 * TB], F32)).ap()
    hst = stk.enter_context(nc.sbuf_tensor("p3b_h", [128, 2 * TB], BF16)).ap()

    pg = [stk.enter_context(nc.psum_tensor(f"p3b_pg{i}", [128, TB], F32))
          for i in range(2)]
    pu = [stk.enter_context(nc.psum_tensor(f"p3b_pu{i}", [128, TB], F32))
          for i in range(2)]

    sGx = S(gstk, nc, "p3b_sGx")
    sSc = S(gstk, nc, "p3b_sSc")
    sW = [S(gstk, nc, "p3b_sW0"), S(gstk, nc, "p3b_sW1")]
    sPp = S(gstk, nc, "p3b_sPp")
    sSi = S(gstk, nc, "p3b_sSi")
    sHv = S(gstk, nc, "p3b_sHv")
    sHw = [S(gstk, nc, "p3b_sHw0"), S(gstk, nc, "p3b_sHw1")]
    env["p3b_sHw"] = sHw[0]
    hw = env["h_scr"]

    with nc.Block() as blk:
        @blk.gpsimd
        def _(gp):
            for col in range(2 * EC):
                gp.indirect_dma_start(
                    xno[:, col * TB:(col + 1) * TB], None,
                    env["xn_scr"],
                    bass.IndirectOffsetOnAxis(sb_ixn[:, col:col + 1], 0)
                ).then_inc(sGx.h, 16)
            sGx.n = 2 * EC * 16
            for fc in range(FC):
                if fc >= 2:
                    gp.wait_ge(sPp.h, 4 * (fc - 1))
                gp.dma_start(
                    out=gch.rearrange("p (c ec f) -> p c ec f", c=2,
                                      f=128)[:, fc % 2],
                    in_=env["gateTr"][fc]).then_inc(sW[fc % 2].h, 16)
                gp.dma_start(
                    out=uch.rearrange("p (c ec f) -> p c ec f", c=2,
                                      f=128)[:, fc % 2],
                    in_=env["upTr"][fc]).then_inc(sW[fc % 2].h, 16)
            sW[0].n = (FC + 1) // 2 * 32
            sW[1].n = FC // 2 * 32

        @blk.sync
        def _(sync):
            sync.dma_start(out=gsb[:, :], in_=env["gate_s"]).then_inc(sSc.h, 16)
            sync.dma_start(out=usb[:, :], in_=env["up_s"]).then_inc(sSc.h, 16)
            sSc.n = 32
            for fc in range(FC):
                for th in range(2):
                    j = fc * 2 + th
                    sync.wait_ge(sHv.h, j + 1)
                    sync.dma_start(out=hw[fc * 128:(fc + 1) * 128,
                                          th * TB:(th + 1) * TB],
                                   in_=hst[:, th * TB:(th + 1) * TB]
                                   ).then_inc(sHw[th].h, 16)
            sHw[0].n = FC * 16
            sHw[1].n = FC * 16

        @blk.tensor
        def _(t):
            t.wait_ge(sGx.h, 2 * EC * 16)
            for fc in range(FC):
                t.wait_ge(sW[fc % 2].h, 32 * (fc // 2 + 1))
                if fc >= 2:
                    t.wait_ge(sSi.h, 2 * (fc - 1))
                    t.wait_ge(sHv.h, 2 * (fc - 1))
                for th in range(2):
                    for ec in range(EC):
                        gl = gch[:, (fc % 2) * EC * 128 + ec * 128:
                                 (fc % 2) * EC * 128 + (ec + 1) * 128]
                        ul = uch[:, (fc % 2) * EC * 128 + ec * 128:
                                 (fc % 2) * EC * 128 + (ec + 1) * 128]
                        rh = xno[:, (th * EC + ec) * TB:(th * EC + ec + 1) * TB]
                        t.matmul(pg[th].ap(), gl, rh, start=(ec == 0),
                                 stop=(ec == EC - 1))
                        i2 = t.matmul(pu[th].ap(), ul, rh, start=(ec == 0),
                                      stop=(ec == EC - 1))
                        if ec == EC - 1:
                            i2.then_inc(sPp.h, 2)
            sPp.n = FC * 4

        @blk.scalar
        def _(sc):
            sc.wait_ge(sSc.h, 32)
            for fc in range(FC):
                for th in range(2):
                    j = fc * 2 + th
                    sc.wait_ge(sPp.h, 4 * fc + 2 * (th + 1))
                    if j >= 2:
                        sc.wait_ge(sHv.h, j - 1)
                    sc.activation(sg[:, th * TB:(th + 1) * TB], pg[th].ap(),
                                  AF.Silu, scale=gsb[:, fc:fc + 1]
                                  ).then_inc(sSi.h, 1)
            sSi.n = FC * 2

        @blk.vector
        def _(v):
            for fc in range(FC):
                for th in range(2):
                    j = fc * 2 + th
                    v.wait_ge(sSi.h, j + 1)
                    if fc >= 1:
                        v.wait_ge(sHw[th].h, 16 * fc)
                    v.scalar_tensor_tensor(
                        hst[:, th * TB:(th + 1) * TB],
                        pu[th].ap(), usb[:, fc:fc + 1],
                        sg[:, th * TB:(th + 1) * TB],
                        ALU.mult, ALU.mult).then_inc(sHv.h, 1)
            sHv.n = FC * 2

    return [sHw[0], sHw[1], sGx, sSc, sW[0], sW[1], sPp, sSi, sHv]



def _phase3c(env):
    nc, stk, geo = env["nc"], env["pstk"], env["geo"]
    gstk = env["gstk"]
    FC = geo["FC"]
    groups = []
    fc0 = 0
    while fc0 < FC:
        gsz = min(8, FC - fc0)
        groups.append((fc0, gsz))
        fc0 += gsz
    NGR = len(groups)
    EH = E // 2

    dch = stk.enter_context(nc.sbuf_tensor("p3c_d", [128, 8 * E], BF16)).ap()
    hch = stk.enter_context(nc.sbuf_tensor("p3c_h", [128, 8 * 2 * TB], BF16)).ap()
    dss = stk.enter_context(nc.sbuf_tensor("p3c_ds", [128, E], F32)).ap()
    od = stk.enter_context(nc.sbuf_tensor("p3c_od", [128, 2 * E], F32)).ap()
    pdh = [stk.enter_context(nc.psum_tensor(f"p3c_pd{i}", [128, EH], F32))
           for i in range(2)]

    sW = S(gstk, nc, "p3c_sW")
    sH = S(gstk, nc, "p3c_sH")
    sDs = S(gstk, nc, "p3c_sDs")
    sPd = S(gstk, nc, "p3c_sPd")
    sEv = S(gstk, nc, "p3c_sEv")
    sAo = [S(gstk, nc, "p3c_sAo0"), S(gstk, nc, "p3c_sAo1")]
    env["p3c_sAo"] = sAo[0]

    with nc.Block() as blk:
        @blk.sync
        def _(sync):
            sync.dma_start(out=dss[:, :], in_=env["down_sb"]).then_inc(sDs.h, 16)
            sDs.n = 16
            for gi, (fc0_, gsz) in enumerate(groups):
                if gi >= 1:
                    sync.wait_ge(sPd.h, 16 * gi)
                for j in range(gsz):
                    fc = fc0_ + j
                    sync.dma_start(out=hch[:, j * 2 * TB:(j + 1) * 2 * TB],
                                   in_=env["h_scr"][fc * 128:(fc + 1) * 128, :]
                                   ).then_inc(sH.h, 16)
            sH.n = sum(g for _, g in groups) * 16

        @blk.gpsimd
        def _(gp):
            for gi, (fc0_, gsz) in enumerate(groups):
                if gi >= 1:
                    gp.wait_ge(sPd.h, 16 * gi)
                for j in range(gsz):
                    fc = fc0_ + j
                    gp.dma_start(out=dch[:, j * E:(j + 1) * E],
                                 in_=env["downT"][fc * 128:(fc + 1) * 128, :]
                                 ).then_inc(sW.h, 16)
                for tt in range(8):
                    j = gi * 8 + tt
                    for eh in range(2):
                        m = j * 2 + eh
                        gp.wait_ge(sEv.h, m + 1)
                        for cq in range(2):
                            gp.dma_start(
                                out=env["out"][tt * 128:(tt + 1) * 128,
                                               eh * EH + cq * 1024:
                                               eh * EH + (cq + 1) * 1024],
                                in_=od[:, (j % 2) * E + eh * EH + cq * 1024:
                                       (j % 2) * E + eh * EH + (cq + 1) * 1024],
                                accum_op=ALU.add).then_inc(sAo[j % 2].h, 16)
            sW.n = sum(g for _, g in groups) * 16
            sAo[0].n = (NGR * 8 + 1) // 2 * 64
            sAo[1].n = NGR * 8 // 2 * 64

        @blk.tensor
        def _(t):
            hcnt = 0
            for gi, (fc0_, gsz) in enumerate(groups):
                hcnt += gsz
                t.wait_ge(sW.h, 16 * hcnt)
                t.wait_ge(sH.h, 16 * hcnt)
                for tt in range(8):
                    j = gi * 8 + tt
                    for eh in range(2):
                        m = j * 2 + eh
                        if m >= 2:
                            t.wait_ge(sEv.h, m - 1)
                        for k in range(gsz):
                            for e4 in range(4):
                                ins = t.matmul(
                                    pdh[eh].ap()[:, e4 * TB:(e4 + 1) * TB],
                                    hch[:, k * 2 * TB + tt * 128:
                                        k * 2 * TB + (tt + 1) * 128],
                                    dch[:, k * E + eh * EH + e4 * TB:
                                        k * E + eh * EH + (e4 + 1) * TB],
                                    start=(k == 0), stop=(k == gsz - 1))
                        ins.then_inc(sPd.h, 1)
            sPd.n = NGR * 16

        @blk.vector
        def _(v):
            v.wait_ge(sDs.h, 16)
            for gi in range(NGR):
                for tt in range(8):
                    j = gi * 8 + tt
                    for eh in range(2):
                        m = j * 2 + eh
                        v.wait_ge(sPd.h, m + 1)
                        if j >= 2:
                            v.wait_ge(sAo[j % 2].h,
                                      64 * (j // 2 - 1) + 32 * (eh + 1))
                        v.tensor_mul(od[:, (j % 2) * E + eh * EH:
                                        (j % 2) * E + (eh + 1) * EH],
                                     pdh[eh].ap(),
                                     dss[:, eh * EH:(eh + 1) * EH]
                                     ).then_inc(sEv.h, 1)
            sEv.n = NGR * 16

    return [sAo[0], sAo[1], sW, sH, sDs, sPd, sEv]



def kernel(L0_attn_x, L0_attn_Wq, L0_attn_Wk, L0_attn_Wv, L0_attn_Wo,
           L0_ffn_gate_w, L0_ffn_gate_s, L0_ffn_up_w, L0_ffn_up_s,
           L0_ffn_down_w, L0_ffn_down_s):
    geo0 = full_geo(0)
    sh = host_shared(geo0, np.asarray(L0_attn_x, np.float32),
                     np.asarray(L0_attn_Wq, np.float32),
                     np.asarray(L0_attn_Wk, np.float32),
                     np.asarray(L0_attn_Wv, np.float32),
                     np.asarray(L0_attn_Wo, np.float32),
                     np.asarray(L0_ffn_gate_w, np.int32),
                     np.asarray(L0_ffn_up_w, np.int32),
                     np.asarray(L0_ffn_down_w, np.int32),
                     np.asarray(L0_ffn_gate_s, np.float32),
                     np.asarray(L0_ffn_up_s, np.float32),
                     np.asarray(L0_ffn_down_s, np.float32))
    x = sh["x_full"]
    cosf, sinf = sh["cos_k"], sh["sin_k"]
    nc = build_program(geo0)
    in_maps = []
    for c in range(NCORE):
        geo = full_geo(c)
        d = dict(sh)
        d.update(host_percore(geo))
        rows = np.concatenate([np.arange(b * TB, (b + 1) * TB)
                               for b in geo["own"]])
        d["x_own"] = np.ascontiguousarray(x[rows])
        d["cos_q"] = np.ascontiguousarray(cosf[:, rows] * np.float32(ISQD))
        d["sin_q"] = np.ascontiguousarray(sinf[:, rows] * np.float32(ISQD))
        in_maps.append(d)
    res = run_bass_kernel_spmd(nc, in_maps, core_ids=list(range(NCORE)))
    outp = np.zeros((1, geo0["T"], E), np.float32)
    for c in range(NCORE):
        geo = full_geo(c)
        o = res.results[c]["out"]
        for a, b in enumerate(geo["own"]):
            outp[0, b * TB:(b + 1) * TB] = o[a * TB:(a + 1) * TB]
    return outp
